# version 06
# baseline (speedup 1.0000x reference)
import contextlib
import os
import numpy as np
import ml_dtypes
import concourse.bass as bass
import concourse.mybir as mybir
from concourse.bass_utils import run_bass_kernel_spmd

F32 = mybir.dt.float32
BF16 = mybir.dt.bfloat16
I32 = mybir.dt.int32
F32R = mybir.dt.float32r
OP = mybir.AluOpType
AF = mybir.ActivationFunctionType
AX = mybir.AxisListType

D = 2048
NPT = 16
DI = 8192
NCORES = 8
EPS = 1e-5
NSLOT = 3


class Sem:
    registry = None

    def __init__(self, es, nc, name):
        self.h = es.enter_context(nc.semaphore(name))
        self.n = 0
        self.name = name
        if Sem.registry is not None:
            Sem.registry.append(self)

    def inc(self, instr, k):
        instr.then_inc(self.h, k)
        self.n += k

    def wait(self, eng, n=None):
        eng.wait_ge(self.h, self.n if n is None else n)


class _FakeInstr:
    def then_inc(self, sem, k):
        return self


class _EngProxy:

    def __init__(self, real, name, ctl):
        self._real = real
        self._name = name
        self._ctl = ctl

    def __getattr__(self, attr):
        fn = getattr(self._real, attr)
        if not callable(fn):
            return fn
        ctl = self._ctl
        name = self._name

        def call(*a, **kw):
            if ctl[0] is None or ctl[0] == name:
                return fn(*a, **kw)
            return _FakeInstr()
        return call


def build_nc(MAIN):
    T1 = 512
    assert MAIN % T1 == 0
    N1 = MAIN // T1
    HALF = MAIN // 2
    T2 = min(512, HALF)
    N2H = HALF // T2
    NS2 = HALF // T2
    NSUB1 = T1 // 128

    nc = bass.Bass("TRN2", target_bir_lowering=False,
                   detect_race_conditions=False)

    xmain = nc.dram_tensor("xmain", [MAIN, D], F32, kind="ExternalInput").ap()
    xpref = nc.dram_tensor("xpref", [NSLOT * MAIN, D], F32, kind="ExternalInput").ap()
    wzT = nc.dram_tensor("wzT", [D, D], F32, kind="ExternalInput").ap()
    wnT = nc.dram_tensor("wnT", [D, D], F32, kind="ExternalInput").ap()
    wgt = nc.dram_tensor("wgt", [4, 4, 4, 128, 16, 128], F32, kind="ExternalInput").ap()
    wut = nc.dram_tensor("wut", [4, 4, 4, 128, 16, 128], F32, kind="ExternalInput").ap()
    wdt = nc.dram_tensor("wdt", [4, 16, 128, 16, 128], F32, kind="ExternalInput").ap()
    bz_in = nc.dram_tensor("bz_in", [128, NPT], F32, kind="ExternalInput").ap()
    bn_in = nc.dram_tensor("bn_in", [128, NPT], F32, kind="ExternalInput").ap()
    ones_in = nc.dram_tensor("ones_in", [128, 128], F32, kind="ExternalInput").ap()
    idf_in = nc.dram_tensor("idf_in", [128, 128], F32, kind="ExternalInput").ap()
    idb_in = nc.dram_tensor("idb_in", [128, 128], BF16, kind="ExternalInput").ap()
    meta = nc.dram_tensor("meta", [1, 4], I32, kind="ExternalInput").ap()
    nonce = nc.dram_tensor("nonce", [1, 16], I32, kind="ExternalInput").ap()
    y = nc.dram_tensor("y", [MAIN, D], F32, kind="ExternalOutput").ap()
    DBG = os.environ.get("KDBG", "0") == "1"
    if DBG:
        d_car = nc.dram_tensor("d_car", [128, 64], F32, kind="ExternalOutput").ap()
        d_par = nc.dram_tensor("d_par", [128, 64], F32, kind="ExternalOutput").ap()
        d_seed = nc.dram_tensor("d_seed", [128, NPT], F32, kind="ExternalOutput").ap()
        d_c = nc.dram_tensor("d_c", [D, MAIN], F32, kind="ExternalOutput").ap()
        d_v = nc.dram_tensor("d_v", [D, MAIN], F32, kind="ExternalOutput").ap()
        d_xm = nc.dram_tensor("d_xm", [MAIN, D], F32, kind="ExternalOutput").ap()
        d_h = nc.dram_tensor("d_h", [128, 16, 512], F32, kind="ExternalOutput").ap()
        d_xn = nc.dram_tensor("d_xn", [128, 16, 512], BF16, kind="ExternalOutput").ap()
        d_col = nc.dram_tensor("d_col", [128, 16], F32, kind="ExternalOutput").ap()

    c_hbm = nc.dram_tensor("c_hbm", [D, MAIN], F32, kind="Internal").ap()
    v_hbm = nc.dram_tensor("v_hbm", [D, MAIN], F32, kind="Internal").ap()
    xm_hbm = nc.dram_tensor("xm_hbm", [D, MAIN], F32, kind="Internal").ap()
    sh_sum = nc.dram_tensor("sh_sum", [2, 128, 64], F32, kind="Internal",
                            addr_space="Shared").ap()
    sh_flag = nc.dram_tensor("sh_flag", [2, 16], I32, kind="Internal",
                             addr_space="Shared").ap()

    c_fm = c_hbm.rearrange("(e p) t -> p e t", p=128)
    v_fm = v_hbm.rearrange("(e p) t -> p e t", p=128)
    xm_fm = xm_hbm.rearrange("(e p) t -> p e t", p=128)

    es = contextlib.ExitStack()
    with es:
        SB = lambda name, shape, dt: es.enter_context(nc.sbuf_tensor(name, shape, dt))
        PSUM = lambda name, shape, dt: es.enter_context(nc.psum_tensor(name, shape, dt))
        all_sems = []
        Sem.registry = all_sems

        def mks(name):
            return Sem(es, nc, name)
        ctl = [None]
        SY = _EngProxy(nc.sync, "sy", ctl)
        SC = _EngProxy(nc.scalar, "sc", ctl)
        VE = _EngProxy(nc.vector, "ve", ctl)
        TE = _EngProxy(nc.tensor, "te", ctl)
        GP = _EngProxy(nc.gpsimd, "gp", ctl)

        idf_sb = SB("idf_sb", [128, 128], F32)
        idb_sb = SB("idb_sb", [128, 128], BF16)
        bz_sb = SB("bz_sb", [128, NPT], F32)
        bn_sb = SB("bn_sb", [128, NPT], F32)
        nbz_sb = SB("nbz_sb", [128, NPT], F32)
        nbn_sb = SB("nbn_sb", [128, NPT], F32)
        nonce_sb = SB("nonce_sb", [1, 16], I32)
        ones_bf = SB("ones_bf", [128, 128], BF16)
        sq_scr = SB("sq_scr", [128, 2, 512], F32)
        mb_sb = SB("mb_sb", [128, 512], F32)
        rb_sb = SB("rb_sb", [128, 512], F32)
        car = SB("car", [128, 64], F32)
        par = sq_scr[:, 1, 0:64]
        seed = SB("seed", [128, NPT], F32)
        car2 = SB("car2", [128, NPT], F32)
        tmp_a = sq_scr[:, 0, 0:NPT]
        tmp_b = sq_scr[:, 0, NPT:2 * NPT]
        eps_c = SB("eps_c", [128, 1], F32)

        sq_bf = None
        xnt = SB("xnt", [128, 2, D], BF16)
        uni_cv = SB("uni_cv", [128, 8, T1], F32)

        p1es = contextlib.ExitStack()
        wz_sb = p1es.enter_context(nc.sbuf_tensor("wz_sb", [128, 16, D], BF16))
        wn_sb = p1es.enter_context(nc.sbuf_tensor("wn_sb", [128, 16, D], BF16))
        xn_fm = p1es.enter_context(
            nc.sbuf_tensor("xn_fm", [128, 2, NPT, T1], BF16))
        tmp3 = p1es.enter_context(nc.sbuf_tensor("tmp3", [128, 3, 512], F32))
        scan_scr = p1es.enter_context(
            nc.sbuf_tensor("scan_scr", [128, 2, 512], F32))

        sq_bf = sq_scr[:].bitcast(BF16)
        zt = tmp3[:, 0, :]
        rt = tmp3[:, 1, :]
        st_ = tmp3[:, 2, :]

        PK0 = PSUM("PK0", [128, 512], F32)
        PK1 = PSUM("PK1", [128, 512], F32)
        PU0 = PSUM("PU0", [128, 512], F32)
        PU1 = PSUM("PU1", [128, 512], F32)
        PSA = PSUM("PSA", [128, 512], F32)
        PSB = PSUM("PSB", [128, 512], F32)
        PTB = PSUM("PTB", [128, 1024], BF16)
        PTF = PSUM("PTF", [128, 512], F32)

        s_const = mks("s_const")
        g_const = mks("g_const")
        v_const = mks("v_const")
        wx = mks("wx")
        bar1 = mks("bar1")
        bar2 = mks("bar2")
        bar3 = [mks("bar3a"), mks("bar3b")]
        x_dn = mks("x_dn")
        x_sd = x_dn
        if DBG:
            dbs = mks("dbs")

        def attn_sems(px, with_spill):
            tp = mks(px + "tp")
            cv = mks(px + "cv")
            scn = mks(px + "scn")
            return dict(
                x=[mks(px + "x0"), mks(px + "x1")],
                sq=mks(px + "sq"),
                tp=tp, ev=tp,
                mk=[mks(px + "mk0"), mks(px + "mk1")],
                mu=[mks(px + "mu0"), mks(px + "mu1")],
                cv=cv, cz=cv,
                spw=(mks(px + "spw") if with_spill else scn), scn=scn,
                sqa=mks(px + "sqa"), stm=mks(px + "stm"),
                stf=mks(px + "stf"), row=mks(px + "row"),
                nrm=mks(px + "nrm"),
            )

        PRE = attn_sems("p_", False)
        MN = attn_sems("m_", True)

        q_cv = [mks("q_cv0"), mks("q_cv1")]
        q_sc = mks("q_sc")
        cv_rel = [None, None]
        q_x = [mks("q_x0"), mks("q_x1")]
        q_sqa = mks("q_sqa")
        q_stm = mks("q_stm")
        q_stf = mks("q_stf")
        q_row = mks("q_row")
        q_cst = mks("q_cst")
        q_tp = mks("q_tp")
        q_ev = q_tp
        q_mid = mks("q_mid")
        q_xm = mks("q_xm")
        q_ln3 = mks("q_ln3")
        f_xr = mks("f_xr")
        f_wg = [mks("f_wg0"), mks("f_wg1")]
        f_wu = [mks("f_wu0"), mks("f_wu1")]
        f_wd = [mks("f_wd0"), mks("f_wd1")]
        wg_rel = [None, None]
        wu_rel = [None, None]
        wd_rel = [None, None, None]
        wd_ld = [None, None, None]
        silu_rel = [None] * 4
        bank_free = {}
        f_mg = [mks("f_mg0"), mks("f_mg1")]
        f_mup = [mks("f_mup0"), mks("f_mup1")]
        f_mgA = f_mg[0]
        f_mupA = f_mup[0]
        f_sil = mks("f_sil")
        f_hp = mks("f_hp")
        f_ab = mks("f_ab")
        f_hs = mks("f_hs")
        f_md = [mks("f_md0"), mks("f_md1")]
        f_fd = [mks("f_fd0"), mks("f_fd1")]
        f_mdA = f_md[0]
        f_fdA = f_fd[0]
        f_nrm4 = mks("f_nrm4")
        f_otp = mks("f_otp")
        f_oev = f_otp
        f_ost = mks("f_ost")

        blk = es.enter_context(nc.Block())

        for dst, src in ((idf_sb, idf_in), (idb_sb, idb_in),
                         (bz_sb, bz_in), (bn_sb, bn_in), (nonce_sb, nonce)):
            s_const.inc(SY.dma_start(out=dst[:], in_=src), 16)
        g_const.inc(GP.dma_start(out=ones_bf[:], in_=ones_in), 16)
        g_const.inc(GP.dma_start(
            out=wz_sb[:], in_=wzT.rearrange("(kc p) e -> p kc e", p=128)), 16)
        g_const.inc(GP.dma_start(
            out=wn_sb[:], in_=wnT.rearrange("(kc p) e -> p kc e", p=128)), 16)

        s_const.wait(VE)
        VE.memset(eps_c[:], EPS)
        VE.tensor_scalar_mul(nbz_sb[:], bz_sb[:], -1.0)
        VE.tensor_scalar_mul(nbn_sb[:], bn_sb[:], -1.0)
        VE.memset(car[:, 0:16], 1.0)
        VE.memset(car[:, 16:32], 0.0)
        VE.memset(car[:, 32:48], 1.0)
        VE.memset(car[:, 48:64], 0.0)
        v_const.inc(VE.drain(), 1)

        regs = {}
        for eng, nm in ((SY, "sy"), (SC, "sc"), (VE, "ve"), (TE, "te"), (GP, "gp")):
            rnp = es.enter_context(eng.register(f"np_{nm}"))
            rpar = es.enter_context(eng.register(f"par_{nm}"))
            eng.load(rnp, meta[0:1, 1:2])
            eng.load(rpar, meta[0:1, 2:3])
            regs[nm] = (rnp, rpar)

        carP_pref, carS_pref = car[:, 0:16], car[:, 16:32]
        carP_main, carS_main = car[:, 32:48], car[:, 48:64]

        def stats_fm(S2, get_tile, n, width, src_bf):
            for e in range(n):
                tsrc = get_tile(e)
                if S2["stm"].n:
                    SC.wait_ge(S2["stm"].h, max(0, S2["stm"].n - 1))
                if e == 0 and S2["row"].n:
                    SC.wait_ge(S2["row"].h, S2["row"].n)
                if src_bf:
                    rhs = tsrc
                else:
                    SC.activation(stat_bf[:, e % 2, 0:width], tsrc, AF.Copy)
                    rhs = stat_bf[:, e % 2, 0:width]
                isq = SC.activation(sq_bf[:, e % 2, 0:width], tsrc, AF.Square)
                S2["sqa"].inc(isq, 1)
                S2["sqa"].wait(TE)
                if e == 0 and S2["stf"].n:
                    S2["stf"].wait(TE)
                TE.matmul(PSA[:, 0:width], ones_bf[:], rhs,
                          start=(e == 0), stop=(e == n - 1))
                i2 = TE.matmul(PSB[:, 0:width], ones_bf[:],
                               sq_bf[:, e % 2, 0:width],
                               start=(e == 0), stop=(e == n - 1))
                S2["stm"].inc(i2, 1)

        def rows_fm(S2, width):
            VE.wait_ge(S2["stm"].h, S2["stm"].n)
            VE.tensor_scalar_mul(mb_sb[:, 0:width], PSA[:, 0:width], 1.0 / D)
            VE.tensor_scalar_mul(rb_sb[:, 0:width], PSB[:, 0:width], 1.0 / D)
            VE.tensor_mul(sq_scr[:, 0, 0:width], mb_sb[:, 0:width],
                          mb_sb[:, 0:width])
            iv = VE.tensor_sub(rb_sb[:, 0:width], rb_sb[:, 0:width],
                               sq_scr[:, 0, 0:width])
            S2["row"].inc(iv, 1)
            S2["row"].wait(SC)
            isc = SC.activation(PSA[:, 0:width], rb_sb[:, 0:width],
                                AF.Sqrt, bias=eps_c[:, 0:1])
            S2["row"].inc(isc, 1)
            S2["row"].wait(VE)
            ivv = VE.reciprocal(rb_sb[:, 0:width], PSA[:, 0:width])
            S2["stf"].inc(ivv, 1)

        def front_pieces(S, x_rows, pr):
            def em_sub(sub):
                xs = S["xsub"]
                S["xsub"] += 1
                sx = xs % 2
                if S["tp"].n >= 2:
                    GP.wait_ge(S["tp"].h, S["tp"].n - 2)
                S["x"][0].inc(GP.dma_start(
                    out=xnt[:, sx, :],
                    in_=x_rows[sub * 128:(sub + 1) * 128, :]), 16)
                S["x"][0].wait(TE)
                for w in range(2):
                    if S["ev"].n:
                        S["ev"].wait(TE)
                    for j in range(8):
                        pt = w * 8 + j
                        ii = TE.transpose(PTB[:, j * 128:(j + 1) * 128],
                                          xnt[:, sx, pt * 128:(pt + 1) * 128],
                                          idb_sb[:])
                    S["tp"].inc(ii, 1)
                    S["tp"].wait(SC)
                    if sub == 0 and w == 0:
                        rel = S["xn_rel"].get(pr)
                        if rel is not None:
                            for sm, cn in rel:
                                SC.wait_ge(sm.h, cn)
                        else:
                            for bs in S["mk"] + S["mu"]:
                                if bs.n:
                                    bs.wait(SC)
                    iev = SC.copy(
                        xn_fm[:, pr, w * 8:(w + 1) * 8,
                              sub * 128:(sub + 1) * 128],
                        PTB[:, 0:1024].rearrange("p (a b) -> p a b", a=8))
                    S["ev"].inc(iev, 1)

            def em_stats(e0, e1):
                def em():
                    if e0 == 0:
                        S["ev"].wait(SC)
                    stats_part(S, pr, e0, e1)
                return em

            def em_norm_rng(e0, e1):
                for e in range(e0, e1):
                    VE.tensor_sub(scan_scr[:, 0, 0:T1],
                                  xn_fm[:, pr, e, :], mb_sb[:, 0:T1])
                    i2 = VE.tensor_mul(xn_fm[:, pr, e, :],
                                       scan_scr[:, 0, 0:T1], rb_sb[:, 0:T1])
                    S["nrm"].inc(i2, 1)

            def em_rows_norm1():
                rows_fm(S, T1)
                S["row"].wait(VE)
                S["nrm_base"][pr] = S["nrm"].n
                em_norm_rng(0, NPT // 2)

            def em_norm2():
                em_norm_rng(NPT // 2, NPT)

            return [lambda s=s: em_sub(s) for s in range(NSUB1)] + [
                em_stats(0, NPT // 2), em_stats(NPT // 2, NPT),
                em_rows_norm1, em_norm2]

        def stats_part(S2, pr, e0, e1):
            for e in range(e0, e1):
                tsrc = xn_fm[:, pr, e, :]
                if S2["stm"].n:
                    SC.wait_ge(S2["stm"].h, max(0, S2["stm"].n - 1))
                if e == 0 and S2["row"].n:
                    SC.wait_ge(S2["row"].h, S2["row"].n)
                isq = SC.activation(sq_bf[:, e % 2, 0:T1], tsrc, AF.Square)
                S2["sqa"].inc(isq, 1)
                S2["sqa"].wait(TE)
                if e == 0 and S2["stf"].n:
                    S2["stf"].wait(TE)
                TE.matmul(PSA[:, 0:T1], ones_bf[:], tsrc,
                          start=(e == 0), stop=(e == NPT - 1))
                i2 = TE.matmul(PSB[:, 0:T1], ones_bf[:],
                               sq_bf[:, e % 2, 0:T1],
                               start=(e == 0), stop=(e == NPT - 1))
                S2["stm"].inc(i2, 1)

        def back_tile(S, pr, nxt, toff_spill, do_spill, carS_ap, carP_ap):
            nb = S["nrm_base"][pr]
            for eg in range(8):
                gpr = eg % 2
                for el in range(2):
                    e = eg * 2 + el
                    bk = e % 2
                    pk = (PK0, PK1)[bk]
                    pu = (PU0, PU1)[bk]
                    c_dst = uni_cv[:, gpr * 2 + el, :]
                    v_dst = uni_cv[:, 4 + gpr * 2 + el, :]
                    if S["cv"].n >= 4:
                        TE.wait_ge(S["cv"].h, S["cv"].n - 3)
                    for kc in range(16):
                        if e == 0:
                            TE.wait_ge(S["nrm"].h, nb + kc + 1)
                        ii = TE.matmul(pk[:, 0:T1],
                                       wz_sb[:, kc, e * 128:(e + 1) * 128],
                                       xn_fm[:, pr, kc, :],
                                       start=(kc == 0), stop=(kc == 15))
                    S["mk"][bk].inc(ii, 1)
                    if S["cv"].n >= 4:
                        TE.wait_ge(S["cv"].h, S["cv"].n - 3)
                    for kc in range(16):
                        ii = TE.matmul(pu[:, 0:T1],
                                       wn_sb[:, kc, e * 128:(e + 1) * 128],
                                       xn_fm[:, pr, kc, :],
                                       start=(kc == 0), stop=(kc == 15))
                    S["mu"][bk].inc(ii, 1)
                    S["mk"][bk].wait(SC)
                    if S["scn"].n >= 2:
                        SC.wait_ge(S["scn"].h, S["scn"].n - 1)
                    if do_spill and S["spw"].n >= 64:
                        SC.wait_ge(S["spw"].h, S["spw"].n - 32)
                    SC.activation(c_dst, pk[:, 0:T1], AF.Sigmoid,
                                  bias=nbz_sb[:, e:e + 1], scale=-1.0)
                    if S["cv"].n:
                        SC.wait_ge(S["cv"].h, S["cv"].n)
                    SC.activation(zt[0:128, 0:T1], pk[:, 0:T1], AF.Sigmoid,
                                  bias=bz_sb[:, e:e + 1])
                    S["mu"][bk].wait(SC)
                    SC.activation(rt[0:128, 0:T1], pu[:, 0:T1], AF.Relu,
                                  bias=bn_sb[:, e:e + 1])
                    SC.activation(st_[0:128, 0:T1], pu[:, 0:T1], AF.Relu,
                                  bias=nbn_sb[:, e:e + 1], scale=-1.0)
                    is2 = SC.activation(st_[0:128, 0:T1], st_[0:128, 0:T1],
                                        AF.Sigmoid, scale=-1.0)
                    S["cz"].inc(is2, 1)
                    S["cz"].wait(VE)
                    if S["scn"].n >= 2:
                        VE.wait_ge(S["scn"].h, S["scn"].n - 1)
                    if do_spill and S["spw"].n >= 64:
                        VE.wait_ge(S["spw"].h, S["spw"].n - 32)
                    VE.tensor_add(rt[0:128, 0:T1], rt[0:128, 0:T1],
                                  st_[0:128, 0:T1])
                    iv = VE.tensor_mul(v_dst, zt[0:128, 0:T1], rt[0:128, 0:T1])
                    S["cv"].inc(iv, 1)
                if do_spill:
                    S["cv"].wait(SY)
                    i1 = SY.dma_start(
                        out=c_fm[:, eg * 2:(eg + 1) * 2,
                                 toff_spill:toff_spill + T1],
                        in_=uni_cv[:, gpr * 2:gpr * 2 + 2, :])
                    S["spw"].inc(i1, 16)
                    i2 = SY.dma_start(
                        out=v_fm[:, eg * 2:(eg + 1) * 2,
                                 toff_spill:toff_spill + T1],
                        in_=uni_cv[:, 4 + gpr * 2:4 + gpr * 2 + 2, :])
                    S["spw"].inc(i2, 16)
                S["cv"].wait(VE)
                for el in range(2):
                    e = eg * 2 + el
                    VE.tensor_tensor_scan(scan_scr[:, el, 0:T1],
                                          uni_cv[:, gpr * 2 + el, :],
                                          uni_cv[:, 4 + gpr * 2 + el, :],
                                          carS_ap[:, e:e + 1],
                                          OP.mult, OP.add)
                VE.drain()
                for el in range(2):
                    e = eg * 2 + el
                    VE.tensor_copy(carS_ap[:, e:e + 1],
                                   scan_scr[:, el, T1 - 1:T1])
                for el in range(2):
                    e = eg * 2 + el
                    VE.tensor_tensor_scan(scan_scr[:, el, 0:T1],
                                          uni_cv[:, gpr * 2 + el, :],
                                          uni_cv[:, gpr * 2 + el, :],
                                          carP_ap[:, e:e + 1],
                                          OP.mult, OP.bypass)
                VE.drain()
                for el in range(2):
                    e = eg * 2 + el
                    VE.tensor_copy(carP_ap[:, e:e + 1],
                                   scan_scr[:, el, T1 - 1:T1])
                ic1 = VE.drain()
                S["scn"].inc(ic1, 1)
                if nxt is not None:
                    nxt[eg]()
            S["xn_rel"][pr] = [(S["mk"][0], S["mk"][0].n),
                               (S["mk"][1], S["mk"][1].n),
                               (S["mu"][0], S["mu"][0].n),
                               (S["mu"][1], S["mu"][1].n)]

        def emit_tiles(S, rows_fn, n_tiles, do_spill, carS_ap, carP_ap):
            S["xsub"] = 0
            S["xn_rel"] = {}
            S["nrm_base"] = {}
            for p in front_pieces(S, rows_fn(0), 0):
                p()
            for tt in range(n_tiles):
                nxt = (front_pieces(S, rows_fn(tt + 1), (tt + 1) % 2)
                       if tt + 1 < n_tiles else None)
                back_tile(S, tt % 2, nxt, tt * T1, do_spill,
                          carS_ap, carP_ap)

        g_const.wait(TE)
        v_const.wait(SC)
        v_const.wait(VE)
        s_const.wait(SC)
        s_const.wait(TE)
        s_const.wait(SY)

        def emit_slot(t):
            emit_tiles(PRE,
                       lambda tt: xpref[t * MAIN + tt * T1:
                                        t * MAIN + (tt + 1) * T1, :],
                       N1, False, carS_pref, carP_pref)

        for t in range(NSLOT):
            passes = (("sy", SY), ("sc", SC), ("ve", VE), ("te", TE),
                      ("gp", GP))
            snap = [s.n for s in all_sems]
            for pi, (nm, eng) in enumerate(passes):
                rnp, _ = regs[nm]
                with eng.register(f"c{t}_{nm}") as cnd:
                    eng.reg_alu(cnd, rnp, t, OP.is_gt)
                    with eng.If(cnd):
                        ctl[0] = nm
                        emit_slot(t)
                        ctl[0] = None
                nc.switch_bb(f"slot{t}_{nm}_join")
                if pi < len(passes) - 1:
                    for s, v in zip(all_sems, snap):
                        s.n = v

        for eng in (SY, SC, VE, TE, GP):
            eng.sem_inc(bar1.h, 1)
        bar1.n = 5
        for eng in (SY, SC, VE, TE, GP):
            bar1.wait(eng)

        emit_tiles(MN, lambda tt: xmain[tt * T1:(tt + 1) * T1, :],
                   N1, True, carS_main, carP_main)

        for eng in (SY, SC, VE, TE, GP):
            eng.sem_inc(bar2.h, 1)
        bar2.n = 5
        for eng in (SY, SC, VE, TE, GP):
            bar2.wait(eng)

        _, rpar_g = regs["gp"]
        with GP.If_eq(rpar_g, 0):
            GP.dma_start(out=sh_sum[0], in_=car[:]).then_inc(wx.h, 16)
        with GP.Else():
            GP.dma_start(out=sh_sum[1], in_=car[:]).then_inc(wx.h, 16)
        wx.n += 16
        wx.wait(GP)
        with GP.If_eq(rpar_g, 0):
            GP.dma_start(out=sh_flag[0:1, :], in_=nonce_sb[:]).then_inc(wx.h, 16)
        with GP.Else():
            GP.dma_start(out=sh_flag[1:2, :], in_=nonce_sb[:]).then_inc(wx.h, 16)
        wx.n += 16
        wx.wait(GP)
        rflg = es.enter_context(GP.register("flg"))
        rnon = es.enter_context(GP.register("non"))
        rnd = es.enter_context(GP.register("nd"))
        GP.load(rnon, nonce[0:1, 0:1])
        GP.reg_mov(rnd, 1)
        with GP.While(rnd):
            with GP.If_eq(rpar_g, 0):
                GP.load(rflg, sh_flag[1:2, 0:1])
            with GP.Else():
                GP.load(rflg, sh_flag[0:1, 0:1])
            GP.reg_alu(rnd, rflg, rnon, OP.not_equal)
        with GP.If_eq(rpar_g, 0):
            GP.dma_start(out=par, in_=sh_sum[1]).then_inc(wx.h, 16)
        with GP.Else():
            GP.dma_start(out=par, in_=sh_sum[0]).then_inc(wx.h, 16)
        wx.n += 16
        wx.wait(GP)
        GP.sem_inc(x_dn.h, 1)
        x_dn.n += 1

        if DBG:
            dbs.inc(SY.dma_start(out=d_car, in_=car[:]), 16)
            dbs.inc(SY.dma_start(out=d_c, in_=c_hbm), 16)
            dbs.inc(SY.dma_start(out=d_v, in_=v_hbm), 16)
        x_dn.wait(VE)
        _, rpar_v = regs["ve"]
        with VE.If_eq(rpar_v, 0):
            VE.tensor_mul(tmp_a, par[:, 0:16], car[:, 16:32])
            VE.drain()
            VE.tensor_add(tmp_a, tmp_a, par[:, 16:32])
            VE.drain()
            VE.tensor_mul(tmp_b, car[:, 0:16], par[:, 0:16])
            VE.drain()
            VE.tensor_scalar_mul(tmp_b, tmp_b, 0.5)
            VE.drain()
            VE.tensor_add(seed[:], tmp_a, tmp_b)
            VE.drain()
            VE.tensor_copy(car2[:], seed[:])
            VE.drain().then_inc(x_sd.h, 1)
        with VE.Else():
            VE.tensor_mul(tmp_a, car[:, 0:16], par[:, 16:32])
            VE.drain()
            VE.tensor_add(tmp_a, tmp_a, car[:, 16:32])
            VE.drain()
            VE.tensor_mul(tmp_b, par[:, 0:16], car[:, 0:16])
            VE.drain()
            VE.tensor_scalar_mul(tmp_b, tmp_b, 0.5)
            VE.drain()
            VE.tensor_add(tmp_a, tmp_a, tmp_b)
            VE.drain()
            VE.tensor_mul(tmp_a, tmp_a, par[:, 32:48])
            VE.drain()
            VE.tensor_add(seed[:], tmp_a, par[:, 48:64])
            VE.drain()
            VE.tensor_copy(car2[:], seed[:])
            VE.drain().then_inc(x_sd.h, 1)
        x_sd.n += 1
        x_sd.wait(VE)
        if DBG:
            x_sd.wait(SY)
            dbs.inc(SY.dma_start(out=d_par, in_=par), 16)
            dbs.inc(SY.dma_start(out=d_seed, in_=seed[:]), 16)

        p1es.close()
        p2es = es.enter_context(contextlib.ExitStack())
        stat_bf = p2es.enter_context(
            nc.sbuf_tensor("stat_bf", [128, 2, 512], BF16))
        uni_h = p2es.enter_context(nc.sbuf_tensor("uni_h", [128, 16, T2], F32))
        xn3 = p2es.enter_context(nc.sbuf_tensor("xn3", [128, NPT, HALF], BF16))
        hs_sc = p2es.enter_context(nc.sbuf_tensor("hs_sc", [128, 16, HALF], BF16))
        wg_sb = p2es.enter_context(nc.sbuf_tensor("wg_sb", [128, 2, 16, 128], BF16))
        wu_sb = p2es.enter_context(nc.sbuf_tensor("wu_sb", [128, 2, 16, 128], BF16))
        wd_sb = p2es.enter_context(nc.sbuf_tensor("wd_sb", [128, 3, 16, 128], BF16))
        uni_f = p2es.enter_context(nc.sbuf_tensor("uni_f", [128, 8, T2], F32))
        mm_t = p2es.enter_context(nc.sbuf_tensor("mm_t", [128, 4, 512], F32))
        hp_t = p2es.enter_context(nc.sbuf_tensor("hp_t", [128, 8, 512], F32))
        xnt_f32 = xnt[:].bitcast(F32)
        spar = [xnt_f32[:, 0, 0:T2], xnt_f32[:, 0, T2:2 * T2]]
        silu_2 = p2es.enter_context(
            nc.sbuf_tensor("silu_2", [128, 2, 512], F32))
        silu_sl = [xnt_f32[:, 1, 0:T2], xnt_f32[:, 1, T2:2 * T2],
                   silu_2[:, 0, 0:T2], silu_2[:, 1, 0:T2]]

        def ffn_ap(ht, sub):
            idx = ht + NPT * sub
            if idx < 8:
                return uni_cv[:, idx, 0:T2]
            if idx < 24:
                return uni_h[:, idx - 8, 0:T2]
            return uni_f[:, idx - 24, 0:T2]

        out_p = [hp_t[:, 0:4, :].rearrange("p a b -> p (a b)")[:, 0:D],
                 hp_t[:, 4:8, :].rearrange("p a b -> p (a b)")[:, 0:D]]


        xn3_rdy = [0] * NS2
        for half in range(2):
            hoff = half * HALF
            b = bar3[half]
            for eng in (SY, SC, VE, TE, GP):
                eng.sem_inc(b.h, 1)
            b.n = 5
            for eng in (SY, SC, VE, TE, GP):
                b.wait(eng)

            MN["spw"].wait(SY)
            Q = dict(sqa=q_sqa, stm=q_stm, stf=q_stf, row=q_row)
            xfm2 = hs_sc[:, :, 0:T2]
            for tt in range(N2H):
                toff = hoff + tt * T2
                if q_sqa.n:
                    VE.wait_ge(q_sqa.h, q_sqa.n)
                if f_otp.n:
                    VE.wait_ge(f_otp.h, f_otp.n)
                for eg in range(8):
                    gpr = eg % 2
                    if cv_rel[gpr] is not None:
                        SY.wait_ge(q_sc.h, cv_rel[gpr])
                    i1 = SY.dma_start(
                        out=uni_cv[:, gpr * 2:gpr * 2 + 2, 0:T2],
                        in_=c_fm[:, eg * 2:(eg + 1) * 2, toff:toff + T2])
                    q_cv[gpr].inc(i1, 16)
                    i2 = SY.dma_start(
                        out=uni_cv[:, 4 + gpr * 2:4 + gpr * 2 + 2, 0:T2],
                        in_=v_fm[:, eg * 2:(eg + 1) * 2, toff:toff + T2])
                    q_cv[gpr].inc(i2, 16)
                    q_cv[gpr].wait(VE)
                    iw = None
                    for el in range(2):
                        e = eg * 2 + el
                        VE.tensor_tensor_scan(uni_h[:, e, 0:T2],
                                              uni_cv[:, gpr * 2 + el, 0:T2],
                                              uni_cv[:, 4 + gpr * 2 + el, 0:T2],
                                              car2[:, e:e + 1], OP.mult, OP.add)
                        VE.drain()
                        VE.tensor_copy(car2[:, e:e + 1],
                                       uni_h[:, e, T2 - 1:T2])
                        iw = VE.drain()
                        q_sc.inc(iw, 1)
                    cv_rel[gpr] = q_sc.n
                for sub in range(T2 // 128):
                    sx = sub % 2
                    if q_tp.n >= 2:
                        GP.wait_ge(q_tp.h, q_tp.n - 2)
                    ix = GP.dma_start(
                        out=xnt[:, sx, :],
                        in_=xmain[toff + sub * 128:toff + (sub + 1) * 128, :])
                    q_x[0].inc(ix, 16)
                    q_x[0].wait(TE)
                    for w in range(2):
                        if q_ev.n:
                            q_ev.wait(TE)
                        for j in range(8):
                            pt = w * 8 + j
                            ii = TE.transpose(PTB[:, j * 128:(j + 1) * 128],
                                              xnt[:, sx,
                                                  pt * 128:(pt + 1) * 128],
                                              idb_sb[:])
                        q_tp.inc(ii, 1)
                        q_tp.wait(SC)
                        if q_mid.n and sub == 0 and w == 0:
                            SC.wait_ge(q_mid.h, q_mid.n)
                        iev = SC.copy(
                            xfm2[:, w * 8:(w + 1) * 8,
                                 sub * 128:(sub + 1) * 128],
                            PTB[:, 0:1024].rearrange("p (a b) -> p a b", a=8))
                        q_ev.inc(iev, 1)
                q_sc.wait(SC)
                stats_fm(Q, lambda e: uni_h[:, e, 0:T2], NPT, T2, src_bf=False)
                rows_fm(Q, T2)
                q_row.wait(VE)
                q_ev.wait(VE)
                imid = None
                for e in range(NPT):
                    VE.tensor_sub(uni_h[:, e, 0:T2], uni_h[:, e, 0:T2],
                                  mb_sb[:, 0:T2])
                    VE.tensor_mul(uni_h[:, e, 0:T2], uni_h[:, e, 0:T2],
                                  rb_sb[:, 0:T2])
                    imid = VE.tensor_add(uni_h[:, e, 0:T2], uni_h[:, e, 0:T2],
                                         xfm2[:, e, 0:T2])
                    q_mid.inc(imid, 1)
                q_mid.wait(SY)
                ixm = SY.dma_start(out=xm_fm[:, :, toff:toff + T2],
                                   in_=uni_h[:, :, 0:T2])
                q_xm.inc(ixm, 16)
                q_mid.wait(SC)
                stats_fm(Q, lambda e: uni_h[:, e, 0:T2], NPT, T2, src_bf=False)
                rows_fm(Q, T2)
                q_row.wait(VE)
                i3 = None
                for e in range(NPT):
                    VE.tensor_sub(mm_t[:, 0, 0:T2], uni_h[:, e, 0:T2],
                                  mb_sb[:, 0:T2])
                    i3 = VE.tensor_mul(xn3[:, e, tt * T2:(tt + 1) * T2],
                                       mm_t[:, 0, 0:T2], rb_sb[:, 0:T2])
                    q_ln3.inc(i3, 1)
                xn3_rdy[tt] = q_ln3.n

            if f_ost.n:
                VE.wait_ge(f_ost.h, f_ost.n)
            bank_free["PSA"] = (q_stf, q_stf.n)
            bank_free["PSB"] = (q_stf, q_stf.n)
            GB = ((PK0, "PK0"), (PK1, "PK1"), (PSA, "PSA"))
            UB = ((PU0, "PU0"), (PU1, "PU1"), (PSB, "PSB"))
            DB = ((PK0, "PK0"), (PK1, "PK1"), (PU0, "PU0"))
            for sc in range(4):
                hs_base = f_hs.n
                for pre in range(3):
                    if wd_rel[pre] is not None:
                        GP.wait_ge(wd_rel[pre][0].h, wd_rel[pre][1])
                    idw = GP.dma_start(out=wd_sb[:, pre], in_=wdt[sc, pre])
                    f_wd[pre % 2].inc(idw, 16)
                    wd_ld[pre] = (f_wd[pre % 2], f_wd[pre % 2].n)
                for qb in range(4):
                    for r in range(4):
                        gi = (sc * 4 + qb) * 4 + r
                        ci = gi % 2
                        pg, gnm = GB[gi % 3]
                        pu, unm = UB[gi % 3]
                        if wg_rel[ci] is not None:
                            GP.wait_ge(wg_rel[ci][0].h, wg_rel[ci][1])
                        ig = GP.dma_start(
                            out=wg_sb[:, ci],
                            in_=wgt[sc, qb, r])
                        f_wg[ci].inc(ig, 16)
                        if wu_rel[ci] is not None:
                            GP.wait_ge(wu_rel[ci][0].h, wu_rel[ci][1])
                        iu = GP.dma_start(
                            out=wu_sb[:, ci],
                            in_=wut[sc, qb, r])
                        f_wu[ci].inc(iu, 16)
                        for sub in range(NS2):
                            slt = (gi % 2) * NS2 + sub
                            TE.wait_ge(q_ln3.h, xn3_rdy[sub])
                            f_wg[ci].wait(TE)
                            bw = bank_free.get(gnm)
                            if bw is not None:
                                TE.wait_ge(bw[0].h, bw[1])
                            for kc in range(16):
                                ii = TE.matmul(
                                    pg[:, 0:T2], wg_sb[:, ci, kc, :],
                                    xn3[:, kc, sub * T2:(sub + 1) * T2],
                                    start=(kc == 0), stop=(kc == 15))
                            f_mgA.inc(ii, 1)
                            mg_rec = f_mgA.n
                            if sub == NS2 - 1:
                                wg_rel[ci] = (f_mgA, f_mgA.n)
                            f_wu[ci].wait(TE)
                            bw = bank_free.get(unm)
                            if bw is not None:
                                TE.wait_ge(bw[0].h, bw[1])
                            for kc in range(16):
                                ii = TE.matmul(
                                    pu[:, 0:T2], wu_sb[:, ci, kc, :],
                                    xn3[:, kc, sub * T2:(sub + 1) * T2],
                                    start=(kc == 0), stop=(kc == 15))
                            f_mupA.inc(ii, 1)
                            mu_rec = f_mupA.n
                            if sub == NS2 - 1:
                                wu_rel[ci] = (f_mupA, f_mupA.n)
                            SC.wait_ge(f_mgA.h, mg_rec)
                            sw = silu_rel[slt]
                            if sw is not None:
                                SC.wait_ge(sw[0].h, sw[1])
                            isl = SC.activation(silu_sl[slt],
                                                pg[:, 0:T2], AF.Silu)
                            f_sil.inc(isl, 1)
                            bank_free[gnm] = (f_sil, f_sil.n)
                            f_sil.wait(VE)
                            VE.wait_ge(f_mupA.h, mu_rec)
                            ihp = VE.tensor_mul(
                                hp_t[:, r * NS2 + sub, 0:T2],
                                pu[:, 0:T2], silu_sl[slt])
                            f_hp.inc(ihp, 1)
                            silu_rel[slt] = (f_hp, f_hp.n)
                            bank_free[unm] = (f_hp, f_hp.n)
                    for sub in range(NS2):
                        f_hp.wait(VE)
                        if f_hs.n:
                            VE.wait_ge(f_hs.h, f_hs.n)
                        if q_stm.n:
                            VE.wait_ge(q_stm.h, q_stm.n)
                        for r in range(4):
                            VE.tensor_mul(mm_t[:, r, 0:T2],
                                          hp_t[:, r * NS2 + sub, 0:T2],
                                          hp_t[:, r * NS2 + sub, 0:T2])
                        VE.tensor_max(spar[0], mm_t[:, 0, 0:T2],
                                      mm_t[:, 1, 0:T2])
                        VE.tensor_tensor(spar[1], mm_t[:, 0, 0:T2],
                                         mm_t[:, 1, 0:T2], OP.min)
                        VE.tensor_max(sq_scr[:, 0, 0:T2], mm_t[:, 2, 0:T2],
                                      mm_t[:, 3, 0:T2])
                        VE.tensor_tensor(sq_scr[:, 1, 0:T2], mm_t[:, 2, 0:T2],
                                         mm_t[:, 3, 0:T2], OP.min)
                        VE.tensor_tensor(spar[0], spar[0],
                                         sq_scr[:, 0, 0:T2], OP.min)
                        VE.tensor_max(spar[1], spar[1],
                                      sq_scr[:, 1, 0:T2])
                        VE.tensor_max(spar[0], spar[0],
                                      spar[1])
                        ihs = None
                        for r in range(4):
                            VE.tensor_tensor(sq_scr[:, 1, 0:T2],
                                             mm_t[:, r, 0:T2],
                                             spar[0], OP.is_ge)
                            ihs = VE.tensor_mul(
                                hs_sc[:, qb * 4 + r, sub * T2:(sub + 1) * T2],
                                hp_t[:, r * NS2 + sub, 0:T2],
                                sq_scr[:, 1, 0:T2])
                        f_hs.inc(ihs, 1)
                for ht in range(16):
                    s4 = ht % 3
                    TE.wait_ge(wd_ld[s4][0].h, wd_ld[s4][1])
                    for sub in range(NS2):
                        di = (sc * 16 + ht) * NS2 + sub
                        pd, dnm = DB[di % 3]
                        bw = bank_free.get(dnm)
                        if bw is not None:
                            TE.wait_ge(bw[0].h, bw[1])
                        if ht == 0 and sub == 0:
                            TE.wait_ge(f_hs.h, hs_base + 2 * NS2)
                            for s in range(8):
                                TE.matmul(
                                    pd[:, 0:T2], wd_sb[:, s4, s, :],
                                    hs_sc[:, s, sub * T2:(sub + 1) * T2],
                                    start=(s == 0), stop=False)
                            TE.wait_ge(f_hs.h, hs_base + 4 * NS2)
                            for s in range(8, 16):
                                ii = TE.matmul(
                                    pd[:, 0:T2], wd_sb[:, s4, s, :],
                                    hs_sc[:, s, sub * T2:(sub + 1) * T2],
                                    start=False, stop=(s == 15))
                        else:
                            for s in range(16):
                                ii = TE.matmul(
                                    pd[:, 0:T2], wd_sb[:, s4, s, :],
                                    hs_sc[:, s, sub * T2:(sub + 1) * T2],
                                    start=(s == 0), stop=(s == 15))
                        f_mdA.inc(ii, 1)
                        md_rec = f_mdA.n
                        VE.wait_ge(f_mdA.h, md_rec)
                        if sc == 0:
                            iac = VE.tensor_copy(ffn_ap(ht, sub),
                                                 pd[:, 0:T2])
                        else:
                            iac = VE.tensor_add(ffn_ap(ht, sub),
                                                pd[:, 0:T2], ffn_ap(ht, sub))
                        f_fdA.inc(iac, 1)
                        bank_free[dnm] = (f_fdA, f_fdA.n)
                    wd_rel[s4] = (f_mdA, f_mdA.n)
                    if ht + 3 < 16:
                        GP.wait_ge(wd_rel[s4][0].h, wd_rel[s4][1])
                        idw = GP.dma_start(out=wd_sb[:, s4],
                                           in_=wdt[sc, ht + 3])
                        f_wd[s4 % 2].inc(idw, 16)
                        wd_ld[s4] = (f_wd[s4 % 2], f_wd[s4 % 2].n)

            q_xm.wait(GP)
            for nm in ("PSA", "PSB"):
                bw = bank_free.get(nm)
                if bw is not None:
                    TE.wait_ge(bw[0].h, bw[1])
            for sub in range(NS2):
                f_fdA.wait(SC)
                stats_fm(Q, lambda e: ffn_ap(e, sub), NPT, T2, src_bf=False)
                rows_fm(Q, T2)
                q_row.wait(VE)
                i2 = None
                for e in range(NPT):
                    VE.tensor_sub(ffn_ap(e, sub), ffn_ap(e, sub),
                                  mb_sb[:, 0:T2])
                    i2 = VE.tensor_mul(ffn_ap(e, sub), ffn_ap(e, sub),
                                       rb_sb[:, 0:T2])
                    f_nrm4.inc(i2, 1)
                f_nrm4.wait(GP)
                idx0 = NPT * sub
                tcol0 = hoff + sub * T2
                if idx0 < 8:
                    ia = GP.dma_start(out=uni_cv[:, 0:8, 0:T2],
                                      in_=xm_fm[:, 0:8, tcol0:tcol0 + T2],
                                      accum_op=OP.add)
                    f_xr.inc(ia, 16)
                    ib = GP.dma_start(out=uni_h[:, 0:8, 0:T2],
                                      in_=xm_fm[:, 8:16, tcol0:tcol0 + T2],
                                      accum_op=OP.add)
                    f_xr.inc(ib, 16)
                else:
                    ia = GP.dma_start(out=uni_h[:, 8:16, 0:T2],
                                      in_=xm_fm[:, 0:8, tcol0:tcol0 + T2],
                                      accum_op=OP.add)
                    f_xr.inc(ia, 16)
                    ib = GP.dma_start(out=uni_f[:, 0:8, 0:T2],
                                      in_=xm_fm[:, 8:16, tcol0:tcol0 + T2],
                                      accum_op=OP.add)
                    f_xr.inc(ib, 16)
                for s2 in range(T2 // 128):
                    gidx = (half * NS2 + sub) * (T2 // 128) + s2
                    tok0 = hoff + sub * T2 + s2 * 128
                    pr = gidx % 2
                    f_xr.wait(TE)
                    q_ev.wait(TE)
                    for w in range(4):
                        if f_oev.n:
                            f_oev.wait(TE)
                        for j in range(4):
                            pt = w * 4 + j
                            ii = TE.transpose(
                                PTF[:, j * 128:(j + 1) * 128],
                                ffn_ap(pt, sub)[:, s2 * 128:(s2 + 1) * 128],
                                idf_sb[:])
                        f_otp.inc(ii, 1)
                        f_otp.wait(VE)
                        if w == 0 and f_ost.n >= 32:
                            VE.wait_ge(f_ost.h, f_ost.n - 16)
                        iev = VE.tensor_copy(
                            out_p[pr].rearrange("p (a b) -> p a b", a=16)
                            [:, w * 4:(w + 1) * 4, :],
                            PTF[:, 0:512].rearrange("p (a b) -> p a b", a=4))
                        f_oev.inc(iev, 1)
                    f_oev.wait(SC)
                    ist = SC.dma_start(out=y[tok0:tok0 + 128, :], in_=out_p[pr])
                    f_ost.inc(ist, 16)

        f_ost.wait(SC)
        for eng in (SY, VE, TE, GP):
            f_ost.wait(eng)

    return nc


def _plane_perm():
    q = np.arange(2048)
    perm = np.empty(8192, np.int64)
    for r in range(4):
        perm[r * 2048:(r + 1) * 2048] = q * 4 + r
    return perm


def _prep_weights(Wz, Wn, wg, wu, wd):
    wzT = np.ascontiguousarray(Wz.T)
    wnT = np.ascontiguousarray(Wn.T)
    perm = _plane_perm()
    wgp = wg[:, perm]
    wup = wu[:, perm]
    wdp = wd[perm, :]

    def strips_gate(w):
        w4 = w.reshape(16, 128, 4, 16, 128)
        w4 = w4.transpose(2, 3, 1, 0, 4)
        w4 = w4.reshape(4, 4, 4, 128, 16, 128)
        w4 = w4.transpose(1, 2, 0, 3, 4, 5)
        return np.ascontiguousarray(w4)

    wgt = strips_gate(wgp)
    wut = strips_gate(wup)
    wdr = wdp.reshape(4, 16, 128, 16, 128)
    wdr = wdr.transpose(1, 0, 2, 3, 4)
    wdr = wdr.reshape(4, 4, 4, 128, 16, 128)
    wdr = wdr.transpose(0, 4, 3, 1, 2, 5)
    wdt = np.ascontiguousarray(wdr.reshape(4, 16, 128, 16, 128))
    return wzT, wnT, wgt, wut, wdt


def _build_and_maps(x2d, Wz, bz, Wn, bn, wg, wu, wd, MAIN):
    L = x2d.shape[0]
    assert L == NCORES * MAIN
    nc = build_nc(MAIN)
    wzT, wnT, wgt, wut, wdt = _prep_weights(Wz, Wn, wg, wu, wd)
    bzt = np.ascontiguousarray(bz.reshape(NPT, 128).T)
    bnt = np.ascontiguousarray(bn.reshape(NPT, 128).T)
    ones = np.ones((128, 128), np.float32)
    idf = np.eye(128, dtype=np.float32)
    idb = np.eye(128, dtype=ml_dtypes.bfloat16)
    nonce_v = np.random.randint(1, 2**31 - 1, dtype=np.int32)
    nonce_arr = np.full((1, 16), nonce_v, np.int32)

    in_maps = []
    for k in range(NCORES):
        j, b = k // 2, k % 2
        xmain = np.ascontiguousarray(x2d[k * MAIN:(k + 1) * MAIN])
        xpref = np.zeros((NSLOT * MAIN, D), np.float32)
        start = b * j * MAIN
        if j > 0:
            xpref[0:j * MAIN] = x2d[start:start + j * MAIN]
        meta_arr = np.array([[k, j, b, 0]], np.int32)
        in_maps.append({
            "xmain": xmain, "xpref": xpref,
            "wzT": wzT, "wnT": wnT, "wgt": wgt, "wut": wut, "wdt": wdt,
            "bz_in": bzt, "bn_in": bnt, "ones_in": ones,
            "idf_in": idf, "idb_in": idb, "meta": meta_arr,
            "nonce": nonce_arr,
        })
    return nc, in_maps


def _run(x2d, Wz, bz, Wn, bn, wg, wu, wd, MAIN):
    L = x2d.shape[0]
    nc, in_maps = _build_and_maps(x2d, Wz, bz, Wn, bn, wg, wu, wd, MAIN)
    res = run_bass_kernel_spmd(nc, in_maps, core_ids=list(range(NCORES)))
    out = np.empty((L, D), np.float32)
    for k in range(NCORES):
        out[k * MAIN:(k + 1) * MAIN] = res.results[k]["y"]
    return out


def kernel(L0_attn_x, L0_attn_Wz, L0_attn_bz, L0_attn_Wn, L0_attn_bn,
           L0_ffn_w_gate, L0_ffn_w_up, L0_ffn_w_down):
    x = np.asarray(L0_attn_x, np.float32)
    B, L, _ = x.shape
    x2d = np.ascontiguousarray(x.reshape(B * L, D))
    out = _run(x2d,
               np.asarray(L0_attn_Wz, np.float32),
               np.asarray(L0_attn_bz, np.float32),
               np.asarray(L0_attn_Wn, np.float32),
               np.asarray(L0_attn_bn, np.float32),
               np.asarray(L0_ffn_w_gate, np.float32),
               np.asarray(L0_ffn_w_up, np.float32),
               np.asarray(L0_ffn_w_down, np.float32),
               MAIN=(B * L) // NCORES)
    return out.reshape(B, L, D)


# version 07
# speedup vs baseline: 1.3227x; 1.3227x over previous
import contextlib
import os
import numpy as np
import ml_dtypes
import concourse.bass as bass
import concourse.mybir as mybir
from concourse.bass_utils import run_bass_kernel_spmd

F32 = mybir.dt.float32
BF16 = mybir.dt.bfloat16
I32 = mybir.dt.int32
F32R = mybir.dt.float32r
OP = mybir.AluOpType
AF = mybir.ActivationFunctionType
AX = mybir.AxisListType

D = 2048
NPT = 16
DI = 8192
NCORES = 8
EPS = 1e-5
NSLOT = 3


class Sem:
    registry = None

    def __init__(self, es, nc, name):
        self.h = es.enter_context(nc.semaphore(name))
        self.n = 0
        self.name = name
        if Sem.registry is not None:
            Sem.registry.append(self)

    def inc(self, instr, k):
        instr.then_inc(self.h, k)
        self.n += k

    def wait(self, eng, n=None):
        eng.wait_ge(self.h, self.n if n is None else n)


class _FakeInstr:
    def then_inc(self, sem, k):
        return self


class _EngProxy:

    def __init__(self, real, name, ctl):
        self._real = real
        self._name = name
        self._ctl = ctl

    def __getattr__(self, attr):
        fn = getattr(self._real, attr)
        if not callable(fn):
            return fn
        ctl = self._ctl
        name = self._name

        def call(*a, **kw):
            if ctl[0] is None or ctl[0] == name:
                return fn(*a, **kw)
            return _FakeInstr()
        return call


def build_nc(MAIN):
    T1 = 512
    assert MAIN % T1 == 0
    N1 = MAIN // T1
    HALF = MAIN // 2
    T2 = min(512, HALF)
    N2H = HALF // T2
    NS2 = HALF // T2
    NSUB1 = T1 // 128

    nc = bass.Bass("TRN2", target_bir_lowering=False,
                   detect_race_conditions=False)

    xmain = nc.dram_tensor("xmain", [MAIN, D], F32, kind="ExternalInput").ap()
    xpref = nc.dram_tensor("xpref", [NSLOT * MAIN, D], F32, kind="ExternalInput").ap()
    wzT = nc.dram_tensor("wzT", [D, D], F32, kind="ExternalInput").ap()
    wnT = nc.dram_tensor("wnT", [D, D], F32, kind="ExternalInput").ap()
    wgt = nc.dram_tensor("wgt", [4, 4, 4, 128, 16, 128], F32, kind="ExternalInput").ap()
    wut = nc.dram_tensor("wut", [4, 4, 4, 128, 16, 128], F32, kind="ExternalInput").ap()
    wdt = nc.dram_tensor("wdt", [4, 16, 128, 16, 128], F32, kind="ExternalInput").ap()
    bz_in = nc.dram_tensor("bz_in", [128, NPT], F32, kind="ExternalInput").ap()
    bn_in = nc.dram_tensor("bn_in", [128, NPT], F32, kind="ExternalInput").ap()
    ones_in = nc.dram_tensor("ones_in", [128, 128], F32, kind="ExternalInput").ap()
    idf_in = nc.dram_tensor("idf_in", [128, 128], F32, kind="ExternalInput").ap()
    idb_in = nc.dram_tensor("idb_in", [128, 128], BF16, kind="ExternalInput").ap()
    meta = nc.dram_tensor("meta", [1, 4], I32, kind="ExternalInput").ap()
    nonce = nc.dram_tensor("nonce", [1, 16], I32, kind="ExternalInput").ap()
    y = nc.dram_tensor("y", [MAIN, D], F32, kind="ExternalOutput").ap()
    DBG = os.environ.get("KDBG", "0") == "1"
    if DBG:
        d_car = nc.dram_tensor("d_car", [128, 64], F32, kind="ExternalOutput").ap()
        d_par = nc.dram_tensor("d_par", [128, 64], F32, kind="ExternalOutput").ap()
        d_seed = nc.dram_tensor("d_seed", [128, NPT], F32, kind="ExternalOutput").ap()
        d_c = nc.dram_tensor("d_c", [D, MAIN], F32, kind="ExternalOutput").ap()
        d_v = nc.dram_tensor("d_v", [D, MAIN], F32, kind="ExternalOutput").ap()
        d_xm = nc.dram_tensor("d_xm", [MAIN, D], F32, kind="ExternalOutput").ap()
        d_h = nc.dram_tensor("d_h", [128, 16, 512], F32, kind="ExternalOutput").ap()
        d_xn = nc.dram_tensor("d_xn", [128, 16, 512], BF16, kind="ExternalOutput").ap()
        d_col = nc.dram_tensor("d_col", [128, 16], F32, kind="ExternalOutput").ap()

    c_hbm = nc.dram_tensor("c_hbm", [D, MAIN], F32, kind="Internal").ap()
    v_hbm = nc.dram_tensor("v_hbm", [D, MAIN], F32, kind="Internal").ap()
    xm_hbm = nc.dram_tensor("xm_hbm", [D, MAIN], F32, kind="Internal").ap()
    sh_sum = nc.dram_tensor("sh_sum", [2, 128, 64], F32, kind="Internal",
                            addr_space="Shared").ap()
    sh_flag = nc.dram_tensor("sh_flag", [2, 16], I32, kind="Internal",
                             addr_space="Shared").ap()

    c_fm = c_hbm.rearrange("(e p) t -> p e t", p=128)
    v_fm = v_hbm.rearrange("(e p) t -> p e t", p=128)
    xm_fm = xm_hbm.rearrange("(e p) t -> p e t", p=128)

    es = contextlib.ExitStack()
    with es:
        SB = lambda name, shape, dt: es.enter_context(nc.sbuf_tensor(name, shape, dt))
        PSUM = lambda name, shape, dt: es.enter_context(nc.psum_tensor(name, shape, dt))
        all_sems = []
        Sem.registry = all_sems

        def mks(name):
            return Sem(es, nc, name)
        ctl = [None]
        SY = _EngProxy(nc.sync, "sy", ctl)
        SC = _EngProxy(nc.scalar, "sc", ctl)
        VE = _EngProxy(nc.vector, "ve", ctl)
        TE = _EngProxy(nc.tensor, "te", ctl)
        GP = _EngProxy(nc.gpsimd, "gp", ctl)

        idf_sb = SB("idf_sb", [128, 128], F32)
        idb_sb = SB("idb_sb", [128, 128], BF16)
        bz_sb = SB("bz_sb", [128, NPT], F32)
        bn_sb = SB("bn_sb", [128, NPT], F32)
        nbz_sb = SB("nbz_sb", [128, NPT], F32)
        nbn_sb = SB("nbn_sb", [128, NPT], F32)
        nonce_sb = SB("nonce_sb", [1, 16], I32)
        ones_bf = SB("ones_bf", [128, 128], BF16)
        stat_bf = SB("stat_bf", [128, 2, 512], BF16)
        sq_scr = SB("sq_scr", [128, 2, 512], F32)
        mb_sb = SB("mb_sb", [128, 512], F32)
        rb_sb = SB("rb_sb", [128, 512], F32)
        car = SB("car", [128, 64], F32)
        par = SB("par", [128, 64], F32)
        seed = SB("seed", [128, NPT], F32)
        car2 = SB("car2", [128, NPT], F32)
        tmp_a = sq_scr[:, 0, 0:NPT]
        tmp_b = sq_scr[:, 0, NPT:2 * NPT]
        colm = SB("colm", [128, 8], F32)
        colr = SB("colr", [128, 8], F32)
        colt = SB("colt", [128, 16], F32)
        sumc = SB("sumc", [128, 8], F32)
        sqc = SB("sqc", [128, 9], F32)
        eps_c = SB("eps_c", [128, 1], F32)

        sq_bf = None
        x_tm = SB("x_tm", [128, 2, D], F32)
        xnt = SB("xnt", [128, D], BF16)
        uni_cv = SB("uni_cv", [128, 8, T1], F32)

        p1es = contextlib.ExitStack()
        wz_sb = p1es.enter_context(nc.sbuf_tensor("wz_sb", [128, 16, D], BF16))
        wn_sb = p1es.enter_context(nc.sbuf_tensor("wn_sb", [128, 16, D], BF16))
        xn_fm = p1es.enter_context(nc.sbuf_tensor("xn_fm", [128, NPT, T1], BF16))
        tmp3 = p1es.enter_context(nc.sbuf_tensor("tmp3", [128, 3, 512], F32))
        p1scr = p1es.enter_context(nc.sbuf_tensor("p1scr", [128, 512], F32))

        sq_bf = sq_scr[:].bitcast(BF16)
        zt = tmp3[:, 0, :]
        rt = tmp3[:, 1, :]
        st_ = tmp3[:, 2, :]
        h_scr = p1scr[:]
        p_scr = p1scr[:]

        PK0 = PSUM("PK0", [128, 512], F32)
        PK1 = PSUM("PK1", [128, 512], F32)
        PU0 = PSUM("PU0", [128, 512], F32)
        PU1 = PSUM("PU1", [128, 512], F32)
        PSA = PSUM("PSA", [128, 512], F32)
        PSB = PSUM("PSB", [128, 512], F32)
        PTB = PSUM("PTB", [128, 1024], BF16)
        PTF = PSUM("PTF", [128, 512], F32)

        s_const = mks("s_const")
        g_const = mks("g_const")
        v_const = mks("v_const")
        wx = mks("wx")
        bar1 = mks("bar1")
        bar2 = mks("bar2")
        bar3 = [mks("bar3a"), mks("bar3b")]
        x_dn = mks("x_dn")
        x_sd = x_dn
        if DBG:
            dbs = mks("dbs")

        def attn_sems(px, with_spill):
            tp = mks(px + "tp")
            cv = mks(px + "cv")
            scn = mks(px + "scn")
            return dict(
                x=[mks(px + "x0"), mks(px + "x1")],
                sq=mks(px + "sq"),
                tp=tp, ev=tp,
                mk=[mks(px + "mk0"), mks(px + "mk1")],
                mu=[mks(px + "mu0"), mks(px + "mu1")],
                cv=cv, cz=cv,
                spw=(mks(px + "spw") if with_spill else scn), scn=scn,
                sqa=mks(px + "sqa"), stm=mks(px + "stm"),
                stf=mks(px + "stf"), row=mks(px + "row"),
                nrm=mks(px + "nrm"),
            )

        PRE = attn_sems("p_", False)
        MN = attn_sems("m_", True)

        q_cv = [mks("q_cv0"), mks("q_cv1")]
        q_sc = mks("q_sc")
        cv_rel = [None, None]
        q_x = [mks("q_x0"), mks("q_x1")]
        q_sqa = mks("q_sqa")
        q_stm = mks("q_stm")
        q_stf = mks("q_stf")
        q_row = mks("q_row")
        q_cst = mks("q_cst")
        q_tp = mks("q_tp")
        q_ev = q_tp
        q_mid = mks("q_mid")
        q_xm = mks("q_xm")
        q_ln3 = mks("q_ln3")
        f_xr = mks("f_xr")
        f_wg = [mks("f_wg0"), mks("f_wg1")]
        f_wu = [mks("f_wu0"), mks("f_wu1")]
        f_wd = [mks("f_wd0"), mks("f_wd1")]
        wg_rel = [None, None]
        wu_rel = [None, None]
        wd_rel = [None, None, None]
        wd_ld = [None, None, None]
        silu_rel = [None] * 4
        bank_free = {}
        f_mg = [mks("f_mg0"), mks("f_mg1")]
        f_mup = [mks("f_mup0"), mks("f_mup1")]
        f_mgA = f_mg[0]
        f_mupA = f_mup[0]
        f_sil = mks("f_sil")
        f_hp = mks("f_hp")
        f_ab = mks("f_ab")
        f_hs = mks("f_hs")
        f_md = [mks("f_md0"), mks("f_md1")]
        f_fd = [mks("f_fd0"), mks("f_fd1")]
        f_mdA = f_md[0]
        f_fdA = f_fd[0]
        f_nrm4 = mks("f_nrm4")
        f_otp = mks("f_otp")
        f_oev = f_otp
        f_ost = mks("f_ost")

        blk = es.enter_context(nc.Block())

        for dst, src in ((idf_sb, idf_in), (idb_sb, idb_in),
                         (bz_sb, bz_in), (bn_sb, bn_in), (nonce_sb, nonce)):
            s_const.inc(SY.dma_start(out=dst[:], in_=src), 16)
        g_const.inc(GP.dma_start(out=ones_bf[:], in_=ones_in), 16)
        g_const.inc(GP.dma_start(
            out=wz_sb[:], in_=wzT.rearrange("(kc p) e -> p kc e", p=128)), 16)
        g_const.inc(GP.dma_start(
            out=wn_sb[:], in_=wnT.rearrange("(kc p) e -> p kc e", p=128)), 16)

        s_const.wait(VE)
        VE.memset(eps_c[:], EPS)
        VE.tensor_scalar_mul(nbz_sb[:], bz_sb[:], -1.0)
        VE.tensor_scalar_mul(nbn_sb[:], bn_sb[:], -1.0)
        VE.memset(car[:, 0:16], 1.0)
        VE.memset(car[:, 16:32], 0.0)
        VE.memset(car[:, 32:48], 1.0)
        VE.memset(car[:, 48:64], 0.0)
        v_const.inc(VE.drain(), 1)

        regs = {}
        for eng, nm in ((SY, "sy"), (SC, "sc"), (VE, "ve"), (TE, "te"), (GP, "gp")):
            rnp = es.enter_context(eng.register(f"np_{nm}"))
            rpar = es.enter_context(eng.register(f"par_{nm}"))
            eng.load(rnp, meta[0:1, 1:2])
            eng.load(rpar, meta[0:1, 2:3])
            regs[nm] = (rnp, rpar)

        carP_pref, carS_pref = car[:, 0:16], car[:, 16:32]
        carP_main, carS_main = car[:, 32:48], car[:, 48:64]

        def stats_fm(S2, get_tile, n, width, src_bf):
            for e in range(n):
                tsrc = get_tile(e)
                if S2["stm"].n:
                    SC.wait_ge(S2["stm"].h, max(0, S2["stm"].n - 1))
                if e == 0 and S2["row"].n:
                    SC.wait_ge(S2["row"].h, S2["row"].n)
                if src_bf:
                    rhs = tsrc
                else:
                    SC.activation(stat_bf[:, e % 2, 0:width], tsrc, AF.Copy)
                    rhs = stat_bf[:, e % 2, 0:width]
                isq = SC.activation(sq_bf[:, e % 2, 0:width], tsrc, AF.Square)
                S2["sqa"].inc(isq, 1)
                S2["sqa"].wait(TE)
                if e == 0 and S2["stf"].n:
                    S2["stf"].wait(TE)
                TE.matmul(PSA[:, 0:width], ones_bf[:], rhs,
                          start=(e == 0), stop=(e == n - 1))
                i2 = TE.matmul(PSB[:, 0:width], ones_bf[:],
                               sq_bf[:, e % 2, 0:width],
                               start=(e == 0), stop=(e == n - 1))
                S2["stm"].inc(i2, 1)

        def rows_fm(S2, width):
            VE.wait_ge(S2["stm"].h, S2["stm"].n)
            VE.tensor_scalar_mul(mb_sb[:, 0:width], PSA[:, 0:width], 1.0 / D)
            VE.tensor_scalar_mul(rb_sb[:, 0:width], PSB[:, 0:width], 1.0 / D)
            VE.tensor_mul(sq_scr[:, 0, 0:width], mb_sb[:, 0:width],
                          mb_sb[:, 0:width])
            iv = VE.tensor_sub(rb_sb[:, 0:width], rb_sb[:, 0:width],
                               sq_scr[:, 0, 0:width])
            S2["row"].inc(iv, 1)
            S2["row"].wait(SC)
            isc = SC.activation(PSA[:, 0:width], rb_sb[:, 0:width],
                                AF.Sqrt, bias=eps_c[:, 0:1])
            S2["row"].inc(isc, 1)
            S2["row"].wait(VE)
            ivv = VE.reciprocal(rb_sb[:, 0:width], PSA[:, 0:width])
            S2["stf"].inc(ivv, 1)

        def attn_tile(S, x_rows, toff_spill, do_spill, carS_ap, carP_ap):
            for bs in S["mk"] + S["mu"]:
                if bs.n:
                    bs.wait(VE)
            for sub in range(NSUB1):
                if S["sq"].n >= 2:
                    SY.wait_ge(S["sq"].h, S["sq"].n - 1)
                S["x"][0].inc(SY.dma_start(
                    out=x_tm[:, sub % 2, :],
                    in_=x_rows[sub * 128:(sub + 1) * 128, :]), 16)
                S["x"][0].wait(SC)
                if S["tp"].n:
                    S["tp"].wait(SC)
                icast = SC.activation(xnt[:], x_tm[:, sub % 2, :], AF.Copy)
                S["sq"].inc(icast, 1)
                S["sq"].wait(TE)
                for w in range(2):
                    if S["ev"].n:
                        S["ev"].wait(TE)
                    for j in range(8):
                        pt = w * 8 + j
                        ii = TE.transpose(PTB[:, j * 128:(j + 1) * 128],
                                          xnt[:, pt * 128:(pt + 1) * 128],
                                          idb_sb[:])
                    S["tp"].inc(ii, 1)
                    S["tp"].wait(VE)
                    iev = VE.tensor_copy(
                        xn_fm[:, w * 8:(w + 1) * 8, sub * 128:(sub + 1) * 128],
                        PTB[:, 0:1024].rearrange("p (a b) -> p a b", a=8))
                    S["ev"].inc(iev, 1)
            S["ev"].wait(SC)
            stats_fm(S, lambda e: xn_fm[:, e, :], NPT, T1, src_bf=True)
            rows_fm(S, T1)
            S["row"].wait(VE)
            for e in range(NPT):
                VE.tensor_sub(p1scr[0:128, 0:T1], xn_fm[:, e, :], mb_sb[:, 0:T1])
                i2 = VE.tensor_mul(xn_fm[:, e, :], p1scr[0:128, 0:T1],
                                   rb_sb[:, 0:T1])
                S["nrm"].inc(i2, 1)
            S["nrm"].wait(TE)
            for eg in range(8):
                gpr = eg % 2
                for el in range(2):
                    e = eg * 2 + el
                    bk = e % 2
                    pk = (PK0, PK1)[bk]
                    pu = (PU0, PU1)[bk]
                    c_dst = uni_cv[:, gpr * 2 + el, :]
                    v_dst = uni_cv[:, 4 + gpr * 2 + el, :]
                    if S["cv"].n >= 4:
                        TE.wait_ge(S["cv"].h, S["cv"].n - 3)
                    for kc in range(16):
                        ii = TE.matmul(pk[:, 0:T1],
                                       wz_sb[:, kc, e * 128:(e + 1) * 128],
                                       xn_fm[:, kc, :],
                                       start=(kc == 0), stop=(kc == 15))
                    S["mk"][bk].inc(ii, 1)
                    if S["cv"].n >= 4:
                        TE.wait_ge(S["cv"].h, S["cv"].n - 3)
                    for kc in range(16):
                        ii = TE.matmul(pu[:, 0:T1],
                                       wn_sb[:, kc, e * 128:(e + 1) * 128],
                                       xn_fm[:, kc, :],
                                       start=(kc == 0), stop=(kc == 15))
                    S["mu"][bk].inc(ii, 1)
                    S["mk"][bk].wait(SC)
                    if S["scn"].n >= 8:
                        SC.wait_ge(S["scn"].h, S["scn"].n - 4)
                    if do_spill and S["spw"].n >= 64:
                        SC.wait_ge(S["spw"].h, S["spw"].n - 32)
                    SC.activation(c_dst, pk[:, 0:T1], AF.Sigmoid,
                                  bias=nbz_sb[:, e:e + 1], scale=-1.0)
                    if S["cv"].n:
                        SC.wait_ge(S["cv"].h, S["cv"].n)
                    SC.activation(zt[0:128, 0:T1], pk[:, 0:T1], AF.Sigmoid,
                                  bias=bz_sb[:, e:e + 1])
                    S["mu"][bk].wait(SC)
                    SC.activation(rt[0:128, 0:T1], pu[:, 0:T1], AF.Relu,
                                  bias=bn_sb[:, e:e + 1])
                    SC.activation(st_[0:128, 0:T1], pu[:, 0:T1], AF.Relu,
                                  bias=nbn_sb[:, e:e + 1], scale=-1.0)
                    is2 = SC.activation(st_[0:128, 0:T1], st_[0:128, 0:T1],
                                        AF.Sigmoid, scale=-1.0)
                    S["cz"].inc(is2, 1)
                    S["cz"].wait(VE)
                    if S["scn"].n >= 8:
                        VE.wait_ge(S["scn"].h, S["scn"].n - 4)
                    if do_spill and S["spw"].n >= 64:
                        VE.wait_ge(S["spw"].h, S["spw"].n - 32)
                    VE.tensor_add(rt[0:128, 0:T1], rt[0:128, 0:T1],
                                  st_[0:128, 0:T1])
                    iv = VE.tensor_mul(v_dst, zt[0:128, 0:T1], rt[0:128, 0:T1])
                    S["cv"].inc(iv, 1)
                if do_spill:
                    S["cv"].wait(SY)
                    i1 = SY.dma_start(
                        out=c_fm[:, eg * 2:(eg + 1) * 2,
                                 toff_spill:toff_spill + T1],
                        in_=uni_cv[:, gpr * 2:gpr * 2 + 2, :])
                    S["spw"].inc(i1, 16)
                    i2 = SY.dma_start(
                        out=v_fm[:, eg * 2:(eg + 1) * 2,
                                 toff_spill:toff_spill + T1],
                        in_=uni_cv[:, 4 + gpr * 2:4 + gpr * 2 + 2, :])
                    S["spw"].inc(i2, 16)
                for el in range(2):
                    e = eg * 2 + el
                    S["cv"].wait(VE)
                    VE.tensor_tensor_scan(h_scr[0:128, 0:T1],
                                          uni_cv[:, gpr * 2 + el, :],
                                          uni_cv[:, 4 + gpr * 2 + el, :],
                                          carS_ap[:, e:e + 1], OP.mult, OP.add)
                    VE.drain()
                    VE.tensor_copy(carS_ap[:, e:e + 1],
                                   h_scr[0:128, T1 - 1:T1])
                    ic1 = VE.drain()
                    S["scn"].inc(ic1, 1)
                    VE.tensor_tensor_scan(p_scr[0:128, 0:T1],
                                          uni_cv[:, gpr * 2 + el, :],
                                          uni_cv[:, gpr * 2 + el, :],
                                          carP_ap[:, e:e + 1], OP.mult, OP.bypass)
                    VE.drain()
                    VE.tensor_copy(carP_ap[:, e:e + 1],
                                   p_scr[0:128, T1 - 1:T1])
                    isc = VE.drain()
                    S["scn"].inc(isc, 1)

        g_const.wait(TE)
        v_const.wait(SC)
        v_const.wait(VE)
        s_const.wait(SC)
        s_const.wait(TE)
        s_const.wait(SY)

        def emit_slot(t):
            for tt in range(N1):
                attn_tile(PRE, xpref[t * MAIN + tt * T1:
                                     t * MAIN + (tt + 1) * T1, :],
                          0, False, carS_pref, carP_pref)

        for t in range(NSLOT):
            passes = (("sy", SY), ("sc", SC), ("ve", VE), ("te", TE))
            snap = [s.n for s in all_sems]
            for pi, (nm, eng) in enumerate(passes):
                rnp, _ = regs[nm]
                with eng.register(f"c{t}_{nm}") as cnd:
                    eng.reg_alu(cnd, rnp, t, OP.is_gt)
                    with eng.If(cnd):
                        ctl[0] = nm
                        emit_slot(t)
                        ctl[0] = None
                nc.switch_bb(f"slot{t}_{nm}_join")
                if pi < len(passes) - 1:
                    for s, v in zip(all_sems, snap):
                        s.n = v

        for eng in (SY, SC, VE, TE, GP):
            eng.sem_inc(bar1.h, 1)
        bar1.n = 5
        for eng in (SY, SC, VE, TE, GP):
            bar1.wait(eng)

        for tt in range(N1):
            attn_tile(MN, xmain[tt * T1:(tt + 1) * T1, :], tt * T1, True,
                      carS_main, carP_main)
            if DBG and tt == 0:
                MN["ev"].wait(SY)
                dbs.inc(SY.dma_start(out=d_xn[:, :, 0:T1], in_=xn_fm[:]), 16)
                dbs.inc(SY.dma_start(out=d_col[:, 0:4], in_=colm[:, 0:4]), 16)
                dbs.inc(SY.dma_start(out=d_col[:, 8:12], in_=colr[:, 0:4]), 16)
                dbs.wait(SC)
                dbs.wait(VE)

        for eng in (SY, SC, VE, TE, GP):
            eng.sem_inc(bar2.h, 1)
        bar2.n = 5
        for eng in (SY, SC, VE, TE, GP):
            bar2.wait(eng)

        _, rpar_g = regs["gp"]
        with GP.If_eq(rpar_g, 0):
            GP.dma_start(out=sh_sum[0], in_=car[:]).then_inc(wx.h, 16)
        with GP.Else():
            GP.dma_start(out=sh_sum[1], in_=car[:]).then_inc(wx.h, 16)
        wx.n += 16
        wx.wait(GP)
        with GP.If_eq(rpar_g, 0):
            GP.dma_start(out=sh_flag[0:1, :], in_=nonce_sb[:]).then_inc(wx.h, 16)
        with GP.Else():
            GP.dma_start(out=sh_flag[1:2, :], in_=nonce_sb[:]).then_inc(wx.h, 16)
        wx.n += 16
        wx.wait(GP)
        rflg = es.enter_context(GP.register("flg"))
        rnon = es.enter_context(GP.register("non"))
        rnd = es.enter_context(GP.register("nd"))
        GP.load(rnon, nonce[0:1, 0:1])
        GP.reg_mov(rnd, 1)
        with GP.While(rnd):
            with GP.If_eq(rpar_g, 0):
                GP.load(rflg, sh_flag[1:2, 0:1])
            with GP.Else():
                GP.load(rflg, sh_flag[0:1, 0:1])
            GP.reg_alu(rnd, rflg, rnon, OP.not_equal)
        with GP.If_eq(rpar_g, 0):
            GP.dma_start(out=par[:], in_=sh_sum[1]).then_inc(wx.h, 16)
        with GP.Else():
            GP.dma_start(out=par[:], in_=sh_sum[0]).then_inc(wx.h, 16)
        wx.n += 16
        wx.wait(GP)
        GP.sem_inc(x_dn.h, 1)
        x_dn.n += 1

        if DBG:
            dbs.inc(SY.dma_start(out=d_car, in_=car[:]), 16)
            dbs.inc(SY.dma_start(out=d_c, in_=c_hbm), 16)
            dbs.inc(SY.dma_start(out=d_v, in_=v_hbm), 16)
        x_dn.wait(VE)
        _, rpar_v = regs["ve"]
        with VE.If_eq(rpar_v, 0):
            VE.tensor_mul(tmp_a, par[:, 0:16], car[:, 16:32])
            VE.drain()
            VE.tensor_add(tmp_a, tmp_a, par[:, 16:32])
            VE.drain()
            VE.tensor_mul(tmp_b, car[:, 0:16], par[:, 0:16])
            VE.drain()
            VE.tensor_scalar_mul(tmp_b, tmp_b, 0.5)
            VE.drain()
            VE.tensor_add(seed[:], tmp_a, tmp_b)
            VE.drain()
            VE.tensor_copy(car2[:], seed[:])
            VE.drain().then_inc(x_sd.h, 1)
        with VE.Else():
            VE.tensor_mul(tmp_a, car[:, 0:16], par[:, 16:32])
            VE.drain()
            VE.tensor_add(tmp_a, tmp_a, car[:, 16:32])
            VE.drain()
            VE.tensor_mul(tmp_b, par[:, 0:16], car[:, 0:16])
            VE.drain()
            VE.tensor_scalar_mul(tmp_b, tmp_b, 0.5)
            VE.drain()
            VE.tensor_add(tmp_a, tmp_a, tmp_b)
            VE.drain()
            VE.tensor_mul(tmp_a, tmp_a, par[:, 32:48])
            VE.drain()
            VE.tensor_add(seed[:], tmp_a, par[:, 48:64])
            VE.drain()
            VE.tensor_copy(car2[:], seed[:])
            VE.drain().then_inc(x_sd.h, 1)
        x_sd.n += 1
        x_sd.wait(VE)
        if DBG:
            x_sd.wait(SY)
            dbs.inc(SY.dma_start(out=d_par, in_=par[:]), 16)
            dbs.inc(SY.dma_start(out=d_seed, in_=seed[:]), 16)

        p1es.close()
        p2es = es.enter_context(contextlib.ExitStack())
        uni_h = p2es.enter_context(nc.sbuf_tensor("uni_h", [128, 16, T2], F32))
        xn3 = p2es.enter_context(nc.sbuf_tensor("xn3", [128, NPT, HALF], BF16))
        hs_sc = p2es.enter_context(nc.sbuf_tensor("hs_sc", [128, 16, HALF], BF16))
        wg_sb = p2es.enter_context(nc.sbuf_tensor("wg_sb", [128, 2, 16, 128], BF16))
        wu_sb = p2es.enter_context(nc.sbuf_tensor("wu_sb", [128, 2, 16, 128], BF16))
        wd_sb = p2es.enter_context(nc.sbuf_tensor("wd_sb", [128, 3, 16, 128], BF16))
        uni_f = p2es.enter_context(nc.sbuf_tensor("uni_f", [128, 8, T2], F32))
        hp_t = p2es.enter_context(nc.sbuf_tensor("hp_t", [128, 8, 512], F32))
        mm_t = x_tm[:, 1, :].rearrange("p (a b) -> p a b", a=4)
        spar = [x_tm[:, 0, 0:T2], x_tm[:, 0, T2:2 * T2]]
        xnt_f32 = xnt[:].bitcast(F32)
        silu_sl = [x_tm[:, 0, 2 * T2:3 * T2], x_tm[:, 0, 3 * T2:4 * T2],
                   xnt_f32[:, 0:T2], xnt_f32[:, T2:2 * T2]]

        def ffn_ap(ht, sub):
            idx = ht + NPT * sub
            if idx < 8:
                return uni_cv[:, idx, 0:T2]
            if idx < 24:
                return uni_h[:, idx - 8, 0:T2]
            return uni_f[:, idx - 24, 0:T2]

        out_p = [hp_t[:, 0:4, :].rearrange("p a b -> p (a b)")[:, 0:D],
                 hp_t[:, 4:8, :].rearrange("p a b -> p (a b)")[:, 0:D]]


        xn3_rdy = [0] * NS2
        for half in range(2):
            hoff = half * HALF
            b = bar3[half]
            for eng in (SY, SC, VE, TE, GP):
                eng.sem_inc(b.h, 1)
            b.n = 5
            for eng in (SY, SC, VE, TE, GP):
                b.wait(eng)

            MN["spw"].wait(SY)
            Q = dict(sqa=q_sqa, stm=q_stm, stf=q_stf, row=q_row)
            xfm2 = hs_sc[:, :, 0:T2]
            for tt in range(N2H):
                toff = hoff + tt * T2
                if q_sqa.n:
                    VE.wait_ge(q_sqa.h, q_sqa.n)
                if f_otp.n:
                    VE.wait_ge(f_otp.h, f_otp.n)
                for eg in range(8):
                    gpr = eg % 2
                    if cv_rel[gpr] is not None:
                        SY.wait_ge(q_sc.h, cv_rel[gpr])
                    i1 = SY.dma_start(
                        out=uni_cv[:, gpr * 2:gpr * 2 + 2, 0:T2],
                        in_=c_fm[:, eg * 2:(eg + 1) * 2, toff:toff + T2])
                    q_cv[gpr].inc(i1, 16)
                    i2 = SY.dma_start(
                        out=uni_cv[:, 4 + gpr * 2:4 + gpr * 2 + 2, 0:T2],
                        in_=v_fm[:, eg * 2:(eg + 1) * 2, toff:toff + T2])
                    q_cv[gpr].inc(i2, 16)
                    q_cv[gpr].wait(VE)
                    iw = None
                    for el in range(2):
                        e = eg * 2 + el
                        VE.tensor_tensor_scan(uni_h[:, e, 0:T2],
                                              uni_cv[:, gpr * 2 + el, 0:T2],
                                              uni_cv[:, 4 + gpr * 2 + el, 0:T2],
                                              car2[:, e:e + 1], OP.mult, OP.add)
                        VE.drain()
                        VE.tensor_copy(car2[:, e:e + 1],
                                       uni_h[:, e, T2 - 1:T2])
                        iw = VE.drain()
                        q_sc.inc(iw, 1)
                    cv_rel[gpr] = q_sc.n
                for sub in range(T2 // 128):
                    if q_cst.n:
                        SY.wait_ge(q_cst.h, q_cst.n)
                    ix = SY.dma_start(
                        out=x_tm[:, 0, :],
                        in_=xmain[toff + sub * 128:toff + (sub + 1) * 128, :])
                    q_x[0].inc(ix, 16)
                    q_x[0].wait(SC)
                    if q_tp.n:
                        q_tp.wait(SC)
                    if q_mid.n and sub == 0:
                        SC.wait_ge(q_mid.h, q_mid.n)
                    icast = SC.activation(xnt[:], x_tm[:, 0, :], AF.Copy)
                    q_cst.inc(icast, 1)
                    q_cst.wait(TE)
                    for w in range(2):
                        if q_ev.n:
                            q_ev.wait(TE)
                        for j in range(8):
                            pt = w * 8 + j
                            ii = TE.transpose(PTB[:, j * 128:(j + 1) * 128],
                                              xnt[:, pt * 128:(pt + 1) * 128],
                                              idb_sb[:])
                        q_tp.inc(ii, 1)
                        q_tp.wait(VE)
                        iev = VE.tensor_copy(
                            xfm2[:, w * 8:(w + 1) * 8,
                                 sub * 128:(sub + 1) * 128],
                            PTB[:, 0:1024].rearrange("p (a b) -> p a b", a=8))
                        q_ev.inc(iev, 1)
                q_sc.wait(SC)
                stats_fm(Q, lambda e: uni_h[:, e, 0:T2], NPT, T2, src_bf=False)
                rows_fm(Q, T2)
                q_row.wait(VE)
                q_ev.wait(VE)
                imid = None
                for e in range(NPT):
                    VE.tensor_sub(uni_h[:, e, 0:T2], uni_h[:, e, 0:T2],
                                  mb_sb[:, 0:T2])
                    VE.tensor_mul(uni_h[:, e, 0:T2], uni_h[:, e, 0:T2],
                                  rb_sb[:, 0:T2])
                    imid = VE.tensor_add(uni_h[:, e, 0:T2], uni_h[:, e, 0:T2],
                                         xfm2[:, e, 0:T2])
                    q_mid.inc(imid, 1)
                q_mid.wait(SY)
                ixm = SY.dma_start(out=xm_fm[:, :, toff:toff + T2],
                                   in_=uni_h[:, :, 0:T2])
                q_xm.inc(ixm, 16)
                q_mid.wait(SC)
                stats_fm(Q, lambda e: uni_h[:, e, 0:T2], NPT, T2, src_bf=False)
                rows_fm(Q, T2)
                q_row.wait(VE)
                i3 = None
                for e in range(NPT):
                    VE.tensor_sub(mm_t[:, 0, 0:T2], uni_h[:, e, 0:T2],
                                  mb_sb[:, 0:T2])
                    i3 = VE.tensor_mul(xn3[:, e, tt * T2:(tt + 1) * T2],
                                       mm_t[:, 0, 0:T2], rb_sb[:, 0:T2])
                    q_ln3.inc(i3, 1)
                xn3_rdy[tt] = q_ln3.n

            if f_ost.n:
                VE.wait_ge(f_ost.h, f_ost.n)
            bank_free["PSA"] = (q_stf, q_stf.n)
            bank_free["PSB"] = (q_stf, q_stf.n)
            GB = ((PK0, "PK0"), (PK1, "PK1"), (PSA, "PSA"))
            UB = ((PU0, "PU0"), (PU1, "PU1"), (PSB, "PSB"))
            DB = ((PK0, "PK0"), (PK1, "PK1"), (PU0, "PU0"))
            for sc in range(4):
                hs_base = f_hs.n
                for pre in range(3):
                    if wd_rel[pre] is not None:
                        GP.wait_ge(wd_rel[pre][0].h, wd_rel[pre][1])
                    idw = GP.dma_start(out=wd_sb[:, pre], in_=wdt[sc, pre])
                    f_wd[pre % 2].inc(idw, 16)
                    wd_ld[pre] = (f_wd[pre % 2], f_wd[pre % 2].n)
                for qb in range(4):
                    for r in range(4):
                        gi = (sc * 4 + qb) * 4 + r
                        ci = gi % 2
                        pg, gnm = GB[gi % 3]
                        pu, unm = UB[gi % 3]
                        if wg_rel[ci] is not None:
                            GP.wait_ge(wg_rel[ci][0].h, wg_rel[ci][1])
                        ig = GP.dma_start(
                            out=wg_sb[:, ci],
                            in_=wgt[sc, qb, r])
                        f_wg[ci].inc(ig, 16)
                        if wu_rel[ci] is not None:
                            GP.wait_ge(wu_rel[ci][0].h, wu_rel[ci][1])
                        iu = GP.dma_start(
                            out=wu_sb[:, ci],
                            in_=wut[sc, qb, r])
                        f_wu[ci].inc(iu, 16)
                        for sub in range(NS2):
                            slt = (gi % 2) * NS2 + sub
                            TE.wait_ge(q_ln3.h, xn3_rdy[sub])
                            f_wg[ci].wait(TE)
                            bw = bank_free.get(gnm)
                            if bw is not None:
                                TE.wait_ge(bw[0].h, bw[1])
                            for kc in range(16):
                                ii = TE.matmul(
                                    pg[:, 0:T2], wg_sb[:, ci, kc, :],
                                    xn3[:, kc, sub * T2:(sub + 1) * T2],
                                    start=(kc == 0), stop=(kc == 15))
                            f_mgA.inc(ii, 1)
                            mg_rec = f_mgA.n
                            if sub == NS2 - 1:
                                wg_rel[ci] = (f_mgA, f_mgA.n)
                            f_wu[ci].wait(TE)
                            bw = bank_free.get(unm)
                            if bw is not None:
                                TE.wait_ge(bw[0].h, bw[1])
                            for kc in range(16):
                                ii = TE.matmul(
                                    pu[:, 0:T2], wu_sb[:, ci, kc, :],
                                    xn3[:, kc, sub * T2:(sub + 1) * T2],
                                    start=(kc == 0), stop=(kc == 15))
                            f_mupA.inc(ii, 1)
                            mu_rec = f_mupA.n
                            if sub == NS2 - 1:
                                wu_rel[ci] = (f_mupA, f_mupA.n)
                            SC.wait_ge(f_mgA.h, mg_rec)
                            sw = silu_rel[slt]
                            if sw is not None:
                                SC.wait_ge(sw[0].h, sw[1])
                            isl = SC.activation(silu_sl[slt],
                                                pg[:, 0:T2], AF.Silu)
                            f_sil.inc(isl, 1)
                            bank_free[gnm] = (f_sil, f_sil.n)
                            f_sil.wait(VE)
                            VE.wait_ge(f_mupA.h, mu_rec)
                            ihp = VE.tensor_mul(
                                hp_t[:, r * NS2 + sub, 0:T2],
                                pu[:, 0:T2], silu_sl[slt])
                            f_hp.inc(ihp, 1)
                            silu_rel[slt] = (f_hp, f_hp.n)
                            bank_free[unm] = (f_hp, f_hp.n)
                    for sub in range(NS2):
                        f_hp.wait(VE)
                        if f_hs.n:
                            VE.wait_ge(f_hs.h, f_hs.n)
                        if q_stm.n:
                            VE.wait_ge(q_stm.h, q_stm.n)
                        for r in range(4):
                            VE.tensor_mul(mm_t[:, r, 0:T2],
                                          hp_t[:, r * NS2 + sub, 0:T2],
                                          hp_t[:, r * NS2 + sub, 0:T2])
                        VE.tensor_max(spar[0], mm_t[:, 0, 0:T2],
                                      mm_t[:, 1, 0:T2])
                        VE.tensor_tensor(spar[1], mm_t[:, 0, 0:T2],
                                         mm_t[:, 1, 0:T2], OP.min)
                        VE.tensor_max(sq_scr[:, 0, 0:T2], mm_t[:, 2, 0:T2],
                                      mm_t[:, 3, 0:T2])
                        VE.tensor_tensor(sq_scr[:, 1, 0:T2], mm_t[:, 2, 0:T2],
                                         mm_t[:, 3, 0:T2], OP.min)
                        VE.tensor_tensor(spar[0], spar[0],
                                         sq_scr[:, 0, 0:T2], OP.min)
                        VE.tensor_max(spar[1], spar[1],
                                      sq_scr[:, 1, 0:T2])
                        VE.tensor_max(spar[0], spar[0],
                                      spar[1])
                        ihs = None
                        for r in range(4):
                            VE.tensor_tensor(sq_scr[:, 1, 0:T2],
                                             mm_t[:, r, 0:T2],
                                             spar[0], OP.is_ge)
                            ihs = VE.tensor_mul(
                                hs_sc[:, qb * 4 + r, sub * T2:(sub + 1) * T2],
                                hp_t[:, r * NS2 + sub, 0:T2],
                                sq_scr[:, 1, 0:T2])
                        f_hs.inc(ihs, 1)
                for ht in range(16):
                    s4 = ht % 3
                    TE.wait_ge(wd_ld[s4][0].h, wd_ld[s4][1])
                    for sub in range(NS2):
                        di = (sc * 16 + ht) * NS2 + sub
                        pd, dnm = DB[di % 3]
                        bw = bank_free.get(dnm)
                        if bw is not None:
                            TE.wait_ge(bw[0].h, bw[1])
                        if ht == 0 and sub == 0:
                            TE.wait_ge(f_hs.h, hs_base + 2 * NS2)
                            for s in range(8):
                                TE.matmul(
                                    pd[:, 0:T2], wd_sb[:, s4, s, :],
                                    hs_sc[:, s, sub * T2:(sub + 1) * T2],
                                    start=(s == 0), stop=False)
                            TE.wait_ge(f_hs.h, hs_base + 4 * NS2)
                            for s in range(8, 16):
                                ii = TE.matmul(
                                    pd[:, 0:T2], wd_sb[:, s4, s, :],
                                    hs_sc[:, s, sub * T2:(sub + 1) * T2],
                                    start=False, stop=(s == 15))
                        else:
                            for s in range(16):
                                ii = TE.matmul(
                                    pd[:, 0:T2], wd_sb[:, s4, s, :],
                                    hs_sc[:, s, sub * T2:(sub + 1) * T2],
                                    start=(s == 0), stop=(s == 15))
                        f_mdA.inc(ii, 1)
                        md_rec = f_mdA.n
                        VE.wait_ge(f_mdA.h, md_rec)
                        if sc == 0:
                            iac = VE.tensor_copy(ffn_ap(ht, sub),
                                                 pd[:, 0:T2])
                        else:
                            iac = VE.tensor_add(ffn_ap(ht, sub),
                                                pd[:, 0:T2], ffn_ap(ht, sub))
                        f_fdA.inc(iac, 1)
                        bank_free[dnm] = (f_fdA, f_fdA.n)
                    wd_rel[s4] = (f_mdA, f_mdA.n)
                    if ht + 3 < 16:
                        GP.wait_ge(wd_rel[s4][0].h, wd_rel[s4][1])
                        idw = GP.dma_start(out=wd_sb[:, s4],
                                           in_=wdt[sc, ht + 3])
                        f_wd[s4 % 2].inc(idw, 16)
                        wd_ld[s4] = (f_wd[s4 % 2], f_wd[s4 % 2].n)

            q_xm.wait(GP)
            for nm in ("PSA", "PSB"):
                bw = bank_free.get(nm)
                if bw is not None:
                    TE.wait_ge(bw[0].h, bw[1])
            for sub in range(NS2):
                f_fdA.wait(SC)
                stats_fm(Q, lambda e: ffn_ap(e, sub), NPT, T2, src_bf=False)
                rows_fm(Q, T2)
                q_row.wait(VE)
                i2 = None
                for e in range(NPT):
                    VE.tensor_sub(ffn_ap(e, sub), ffn_ap(e, sub),
                                  mb_sb[:, 0:T2])
                    i2 = VE.tensor_mul(ffn_ap(e, sub), ffn_ap(e, sub),
                                       rb_sb[:, 0:T2])
                    f_nrm4.inc(i2, 1)
                f_nrm4.wait(GP)
                idx0 = NPT * sub
                tcol0 = hoff + sub * T2
                if idx0 < 8:
                    ia = GP.dma_start(out=uni_cv[:, 0:8, 0:T2],
                                      in_=xm_fm[:, 0:8, tcol0:tcol0 + T2],
                                      accum_op=OP.add)
                    f_xr.inc(ia, 16)
                    ib = GP.dma_start(out=uni_h[:, 0:8, 0:T2],
                                      in_=xm_fm[:, 8:16, tcol0:tcol0 + T2],
                                      accum_op=OP.add)
                    f_xr.inc(ib, 16)
                else:
                    ia = GP.dma_start(out=uni_h[:, 8:16, 0:T2],
                                      in_=xm_fm[:, 0:8, tcol0:tcol0 + T2],
                                      accum_op=OP.add)
                    f_xr.inc(ia, 16)
                    ib = GP.dma_start(out=uni_f[:, 0:8, 0:T2],
                                      in_=xm_fm[:, 8:16, tcol0:tcol0 + T2],
                                      accum_op=OP.add)
                    f_xr.inc(ib, 16)
                for s2 in range(T2 // 128):
                    gidx = (half * NS2 + sub) * (T2 // 128) + s2
                    tok0 = hoff + sub * T2 + s2 * 128
                    pr = gidx % 2
                    f_xr.wait(TE)
                    q_ev.wait(TE)
                    for w in range(4):
                        if f_oev.n:
                            f_oev.wait(TE)
                        for j in range(4):
                            pt = w * 4 + j
                            ii = TE.transpose(
                                PTF[:, j * 128:(j + 1) * 128],
                                ffn_ap(pt, sub)[:, s2 * 128:(s2 + 1) * 128],
                                idf_sb[:])
                        f_otp.inc(ii, 1)
                        f_otp.wait(VE)
                        if w == 0 and f_ost.n >= 32:
                            VE.wait_ge(f_ost.h, f_ost.n - 16)
                        iev = VE.tensor_copy(
                            out_p[pr].rearrange("p (a b) -> p a b", a=16)
                            [:, w * 4:(w + 1) * 4, :],
                            PTF[:, 0:512].rearrange("p (a b) -> p a b", a=4))
                        f_oev.inc(iev, 1)
                    f_oev.wait(SC)
                    ist = SC.dma_start(out=y[tok0:tok0 + 128, :], in_=out_p[pr])
                    f_ost.inc(ist, 16)

        f_ost.wait(SC)
        for eng in (SY, VE, TE, GP):
            f_ost.wait(eng)

    return nc


def _plane_perm():
    q = np.arange(2048)
    perm = np.empty(8192, np.int64)
    for r in range(4):
        perm[r * 2048:(r + 1) * 2048] = q * 4 + r
    return perm


def _prep_weights(Wz, Wn, wg, wu, wd):
    wzT = np.ascontiguousarray(Wz.T)
    wnT = np.ascontiguousarray(Wn.T)
    perm = _plane_perm()
    wgp = wg[:, perm]
    wup = wu[:, perm]
    wdp = wd[perm, :]

    def strips_gate(w):
        w4 = w.reshape(16, 128, 4, 16, 128)
        w4 = w4.transpose(2, 3, 1, 0, 4)
        w4 = w4.reshape(4, 4, 4, 128, 16, 128)
        w4 = w4.transpose(1, 2, 0, 3, 4, 5)
        return np.ascontiguousarray(w4)

    wgt = strips_gate(wgp)
    wut = strips_gate(wup)
    wdr = wdp.reshape(4, 16, 128, 16, 128)
    wdr = wdr.transpose(1, 0, 2, 3, 4)
    wdr = wdr.reshape(4, 4, 4, 128, 16, 128)
    wdr = wdr.transpose(0, 4, 3, 1, 2, 5)
    wdt = np.ascontiguousarray(wdr.reshape(4, 16, 128, 16, 128))
    return wzT, wnT, wgt, wut, wdt


def _build_and_maps(x2d, Wz, bz, Wn, bn, wg, wu, wd, MAIN):
    L = x2d.shape[0]
    assert L == NCORES * MAIN
    nc = build_nc(MAIN)
    wzT, wnT, wgt, wut, wdt = _prep_weights(Wz, Wn, wg, wu, wd)
    bzt = np.ascontiguousarray(bz.reshape(NPT, 128).T)
    bnt = np.ascontiguousarray(bn.reshape(NPT, 128).T)
    ones = np.ones((128, 128), np.float32)
    idf = np.eye(128, dtype=np.float32)
    idb = np.eye(128, dtype=ml_dtypes.bfloat16)
    nonce_v = np.random.randint(1, 2**31 - 1, dtype=np.int32)
    nonce_arr = np.full((1, 16), nonce_v, np.int32)

    in_maps = []
    for k in range(NCORES):
        j, b = k // 2, k % 2
        xmain = np.ascontiguousarray(x2d[k * MAIN:(k + 1) * MAIN])
        xpref = np.zeros((NSLOT * MAIN, D), np.float32)
        start = b * j * MAIN
        if j > 0:
            xpref[0:j * MAIN] = x2d[start:start + j * MAIN]
        meta_arr = np.array([[k, j, b, 0]], np.int32)
        in_maps.append({
            "xmain": xmain, "xpref": xpref,
            "wzT": wzT, "wnT": wnT, "wgt": wgt, "wut": wut, "wdt": wdt,
            "bz_in": bzt, "bn_in": bnt, "ones_in": ones,
            "idf_in": idf, "idb_in": idb, "meta": meta_arr,
            "nonce": nonce_arr,
        })
    return nc, in_maps


def _run(x2d, Wz, bz, Wn, bn, wg, wu, wd, MAIN):
    L = x2d.shape[0]
    nc, in_maps = _build_and_maps(x2d, Wz, bz, Wn, bn, wg, wu, wd, MAIN)
    res = run_bass_kernel_spmd(nc, in_maps, core_ids=list(range(NCORES)))
    out = np.empty((L, D), np.float32)
    for k in range(NCORES):
        out[k * MAIN:(k + 1) * MAIN] = res.results[k]["y"]
    return out


def kernel(L0_attn_x, L0_attn_Wz, L0_attn_bz, L0_attn_Wn, L0_attn_bn,
           L0_ffn_w_gate, L0_ffn_w_up, L0_ffn_w_down):
    x = np.asarray(L0_attn_x, np.float32)
    B, L, _ = x.shape
    x2d = np.ascontiguousarray(x.reshape(B * L, D))
    out = _run(x2d,
               np.asarray(L0_attn_Wz, np.float32),
               np.asarray(L0_attn_bz, np.float32),
               np.asarray(L0_attn_Wn, np.float32),
               np.asarray(L0_attn_bn, np.float32),
               np.asarray(L0_ffn_w_gate, np.float32),
               np.asarray(L0_ffn_w_up, np.float32),
               np.asarray(L0_ffn_w_down, np.float32),
               MAIN=(B * L) // NCORES)
    return out.reshape(B, L, D)


# version 08
# speedup vs baseline: 2.1623x; 1.6347x over previous
import contextlib
import os
import numpy as np
import ml_dtypes
import concourse.bass as bass
import concourse.mybir as mybir
from concourse.bass_utils import run_bass_kernel_spmd

F32 = mybir.dt.float32
BF16 = mybir.dt.bfloat16
I32 = mybir.dt.int32
F32R = mybir.dt.float32r
OP = mybir.AluOpType
AF = mybir.ActivationFunctionType
AX = mybir.AxisListType

D = 2048
NPT = 16
DI = 8192
NCORES = 8
EPS = 1e-5
NSLOT = 3


class Sem:
    registry = None

    def __init__(self, es, nc, name):
        self.h = es.enter_context(nc.semaphore(name))
        self.n = 0
        self.name = name
        if Sem.registry is not None:
            Sem.registry.append(self)

    def inc(self, instr, k):
        instr.then_inc(self.h, k)
        self.n += k

    def wait(self, eng, n=None):
        eng.wait_ge(self.h, self.n if n is None else n)


class _FakeInstr:
    def then_inc(self, sem, k):
        return self


class _EngProxy:

    def __init__(self, real, name, ctl):
        self._real = real
        self._name = name
        self._ctl = ctl

    def __getattr__(self, attr):
        fn = getattr(self._real, attr)
        if not callable(fn):
            return fn
        ctl = self._ctl
        name = self._name

        def call(*a, **kw):
            if ctl[0] is None or ctl[0] == name:
                return fn(*a, **kw)
            return _FakeInstr()
        return call


def build_nc(MAIN):
    T1 = 512
    assert MAIN % T1 == 0
    N1 = MAIN // T1
    HALF = MAIN // 2
    T2 = min(512, HALF)
    N2H = HALF // T2
    NS2 = HALF // T2
    NSUB1 = T1 // 128

    nc = bass.Bass("TRN2", target_bir_lowering=False,
                   detect_race_conditions=False)

    xmain = nc.dram_tensor("xmain", [MAIN, D], F32, kind="ExternalInput").ap()
    xpref = nc.dram_tensor("xpref", [NSLOT * MAIN, D], F32, kind="ExternalInput").ap()
    wzT = nc.dram_tensor("wzT", [D, D], F32, kind="ExternalInput").ap()
    wnT = nc.dram_tensor("wnT", [D, D], F32, kind="ExternalInput").ap()
    wgt = nc.dram_tensor("wgt", [4, 4, 4, 128, 16, 128], F32, kind="ExternalInput").ap()
    wut = nc.dram_tensor("wut", [4, 4, 4, 128, 16, 128], F32, kind="ExternalInput").ap()
    wdt = nc.dram_tensor("wdt", [4, 16, 128, 16, 128], F32, kind="ExternalInput").ap()
    bz_in = nc.dram_tensor("bz_in", [128, NPT], F32, kind="ExternalInput").ap()
    bn_in = nc.dram_tensor("bn_in", [128, NPT], F32, kind="ExternalInput").ap()
    ones_in = nc.dram_tensor("ones_in", [128, 128], F32, kind="ExternalInput").ap()
    idf_in = nc.dram_tensor("idf_in", [128, 128], F32, kind="ExternalInput").ap()
    idb_in = nc.dram_tensor("idb_in", [128, 128], BF16, kind="ExternalInput").ap()
    meta = nc.dram_tensor("meta", [1, 4], I32, kind="ExternalInput").ap()
    nonce = nc.dram_tensor("nonce", [1, 16], I32, kind="ExternalInput").ap()
    y = nc.dram_tensor("y", [MAIN, D], F32, kind="ExternalOutput").ap()
    DBG = os.environ.get("KDBG", "0") == "1"
    if DBG:
        d_car = nc.dram_tensor("d_car", [128, 64], F32, kind="ExternalOutput").ap()
        d_par = nc.dram_tensor("d_par", [128, 64], F32, kind="ExternalOutput").ap()
        d_seed = nc.dram_tensor("d_seed", [128, NPT], F32, kind="ExternalOutput").ap()
        d_c = nc.dram_tensor("d_c", [D, MAIN], F32, kind="ExternalOutput").ap()
        d_v = nc.dram_tensor("d_v", [D, MAIN], F32, kind="ExternalOutput").ap()
        d_xm = nc.dram_tensor("d_xm", [MAIN, D], F32, kind="ExternalOutput").ap()
        d_h = nc.dram_tensor("d_h", [128, 16, 512], F32, kind="ExternalOutput").ap()
        d_xn = nc.dram_tensor("d_xn", [128, 16, 512], BF16, kind="ExternalOutput").ap()
        d_col = nc.dram_tensor("d_col", [128, 16], F32, kind="ExternalOutput").ap()

    c_hbm = nc.dram_tensor("c_hbm", [D, MAIN], F32, kind="Internal").ap()
    v_hbm = nc.dram_tensor("v_hbm", [D, MAIN], F32, kind="Internal").ap()
    xm_hbm = nc.dram_tensor("xm_hbm", [D, MAIN], F32, kind="Internal").ap()
    sh_sum = nc.dram_tensor("sh_sum", [2, 128, 64], F32, kind="Internal",
                            addr_space="Shared").ap()
    sh_flag = nc.dram_tensor("sh_flag", [2, 16], I32, kind="Internal",
                             addr_space="Shared").ap()

    c_fm = c_hbm.rearrange("(e p) t -> p e t", p=128)
    v_fm = v_hbm.rearrange("(e p) t -> p e t", p=128)
    xm_fm = xm_hbm.rearrange("(e p) t -> p e t", p=128)

    es = contextlib.ExitStack()
    with es:
        SB = lambda name, shape, dt: es.enter_context(nc.sbuf_tensor(name, shape, dt))
        PSUM = lambda name, shape, dt: es.enter_context(nc.psum_tensor(name, shape, dt))
        all_sems = []
        Sem.registry = all_sems

        def mks(name):
            return Sem(es, nc, name)
        ctl = [None]
        SY = _EngProxy(nc.sync, "sy", ctl)
        SC = _EngProxy(nc.scalar, "sc", ctl)
        VE = _EngProxy(nc.vector, "ve", ctl)
        TE = _EngProxy(nc.tensor, "te", ctl)
        GP = _EngProxy(nc.gpsimd, "gp", ctl)

        idf_sb = SB("idf_sb", [128, 128], F32)
        idb_sb = SB("idb_sb", [128, 128], BF16)
        bz_sb = SB("bz_sb", [128, NPT], F32)
        bn_sb = SB("bn_sb", [128, NPT], F32)
        nbz_sb = SB("nbz_sb", [128, NPT], F32)
        nbn_sb = SB("nbn_sb", [128, NPT], F32)
        nonce_sb = SB("nonce_sb", [1, 16], I32)
        ones_bf = SB("ones_bf", [128, 128], BF16)
        stat_bf = SB("stat_bf", [128, 2, 512], BF16)
        sq_scr = SB("sq_scr", [128, 2, 512], F32)
        mb_sb = SB("mb_sb", [128, 512], F32)
        rb_sb = SB("rb_sb", [128, 512], F32)
        car = SB("car", [128, 64], F32)
        par = SB("par", [128, 64], F32)
        seed = SB("seed", [128, NPT], F32)
        car2 = SB("car2", [128, NPT], F32)
        tmp_a = sq_scr[:, 0, 0:NPT]
        tmp_b = sq_scr[:, 0, NPT:2 * NPT]
        colm = SB("colm", [128, 8], F32)
        colr = SB("colr", [128, 8], F32)
        colt = SB("colt", [128, 16], F32)
        sumc = SB("sumc", [128, 8], F32)
        sqc = SB("sqc", [128, 9], F32)
        eps_c = SB("eps_c", [128, 1], F32)

        sq_bf = None
        x_tm = SB("x_tm", [128, 1, D], F32)
        xnt = SB("xnt", [128, D], BF16)
        uni_cv = SB("uni_cv", [128, 8, T1], F32)

        p1es = contextlib.ExitStack()
        wz_sb = p1es.enter_context(nc.sbuf_tensor("wz_sb", [128, 16, D], BF16))
        wn_sb = p1es.enter_context(nc.sbuf_tensor("wn_sb", [128, 16, D], BF16))
        xn_fm = p1es.enter_context(nc.sbuf_tensor("xn_fm", [128, NPT, T1], BF16))
        tmp3 = p1es.enter_context(nc.sbuf_tensor("tmp3", [128, 3, 512], F32))
        p1scr = p1es.enter_context(nc.sbuf_tensor("p1scr", [128, 512], F32))

        sq_bf = sq_scr[:].bitcast(BF16)
        zt = tmp3[:, 0, :]
        rt = tmp3[:, 1, :]
        st_ = tmp3[:, 2, :]
        h_scr = p1scr[:]
        p_scr = p1scr[:]

        PK0 = PSUM("PK0", [128, 512], F32)
        PK1 = PSUM("PK1", [128, 512], F32)
        PU0 = PSUM("PU0", [128, 512], F32)
        PU1 = PSUM("PU1", [128, 512], F32)
        PSA = PSUM("PSA", [128, 512], F32)
        PSB = PSUM("PSB", [128, 512], F32)
        PTB = PSUM("PTB", [128, 1024], BF16)
        PTF = PSUM("PTF", [128, 512], F32)

        s_const = mks("s_const")
        g_const = mks("g_const")
        v_const = mks("v_const")
        wx = mks("wx")
        bar1 = mks("bar1")
        bar2 = mks("bar2")
        bar3 = [mks("bar3a"), mks("bar3b")]
        x_dn = mks("x_dn")
        x_sd = x_dn
        if DBG:
            dbs = mks("dbs")

        def attn_sems(px, with_spill):
            tp = mks(px + "tp")
            cv = mks(px + "cv")
            scn = mks(px + "scn")
            return dict(
                x=[mks(px + "x0"), mks(px + "x1")],
                sq=mks(px + "sq"),
                tp=tp, ev=tp,
                mk=[mks(px + "mk0"), mks(px + "mk1")],
                mu=[mks(px + "mu0"), mks(px + "mu1")],
                cv=cv, cz=cv,
                spw=(mks(px + "spw") if with_spill else scn), scn=scn,
                sqa=mks(px + "sqa"), stm=mks(px + "stm"),
                stf=mks(px + "stf"), row=mks(px + "row"),
                nrm=mks(px + "nrm"),
            )

        PRE = attn_sems("p_", False)
        MN = attn_sems("m_", True)

        q_cv = [mks("q_cv0"), mks("q_cv1")]
        q_sc = mks("q_sc")
        cv_rel = [None, None]
        q_x = [mks("q_x0"), mks("q_x1")]
        q_sqa = mks("q_sqa")
        q_stm = mks("q_stm")
        q_stf = mks("q_stf")
        q_row = mks("q_row")
        q_cst = mks("q_cst")
        q_tp = mks("q_tp")
        q_ev = q_tp
        q_mid = mks("q_mid")
        q_xm = mks("q_xm")
        q_ln3 = mks("q_ln3")
        f_xr = mks("f_xr")
        f_wg = [mks("f_wg0"), mks("f_wg1")]
        f_wu = [mks("f_wu0"), mks("f_wu1")]
        f_wd = [mks("f_wd0"), mks("f_wd1")]
        wg_rel = [None, None]
        wu_rel = [None, None]
        wd_rel = [None, None, None]
        wd_ld = [None, None, None]
        silu_rel = [None] * 4
        bank_free = {}
        f_mg = [mks("f_mg0"), mks("f_mg1")]
        f_mup = [mks("f_mup0"), mks("f_mup1")]
        f_mgA = f_mg[0]
        f_mupA = f_mup[0]
        f_sil = mks("f_sil")
        f_hp = mks("f_hp")
        f_ab = mks("f_ab")
        f_hs = mks("f_hs")
        f_md = [mks("f_md0"), mks("f_md1")]
        f_fd = [mks("f_fd0"), mks("f_fd1")]
        f_mdA = f_md[0]
        f_fdA = f_fd[0]
        f_nrm4 = mks("f_nrm4")
        f_otp = mks("f_otp")
        f_oev = f_otp
        f_ost = mks("f_ost")

        blk = es.enter_context(nc.Block())

        for dst, src in ((idf_sb, idf_in), (idb_sb, idb_in),
                         (bz_sb, bz_in), (bn_sb, bn_in), (nonce_sb, nonce)):
            s_const.inc(SY.dma_start(out=dst[:], in_=src), 16)
        g_const.inc(GP.dma_start(out=ones_bf[:], in_=ones_in), 16)
        g_const.inc(GP.dma_start(
            out=wz_sb[:], in_=wzT.rearrange("(kc p) e -> p kc e", p=128)), 16)
        g_const.inc(GP.dma_start(
            out=wn_sb[:], in_=wnT.rearrange("(kc p) e -> p kc e", p=128)), 16)

        s_const.wait(VE)
        VE.memset(eps_c[:], EPS)
        VE.tensor_scalar_mul(nbz_sb[:], bz_sb[:], -1.0)
        VE.tensor_scalar_mul(nbn_sb[:], bn_sb[:], -1.0)
        VE.memset(car[:, 0:16], 1.0)
        VE.memset(car[:, 16:32], 0.0)
        VE.memset(car[:, 32:48], 1.0)
        VE.memset(car[:, 48:64], 0.0)
        v_const.inc(VE.drain(), 1)

        regs = {}
        for eng, nm in ((SY, "sy"), (SC, "sc"), (VE, "ve"), (TE, "te"), (GP, "gp")):
            rnp = es.enter_context(eng.register(f"np_{nm}"))
            rpar = es.enter_context(eng.register(f"par_{nm}"))
            eng.load(rnp, meta[0:1, 1:2])
            eng.load(rpar, meta[0:1, 2:3])
            regs[nm] = (rnp, rpar)

        carP_pref, carS_pref = car[:, 0:16], car[:, 16:32]
        carP_main, carS_main = car[:, 32:48], car[:, 48:64]

        def stats_fm(S2, get_tile, n, width, src_bf):
            for e in range(n):
                tsrc = get_tile(e)
                if S2["stm"].n:
                    SC.wait_ge(S2["stm"].h, max(0, S2["stm"].n - 1))
                if e == 0 and S2["row"].n:
                    SC.wait_ge(S2["row"].h, S2["row"].n)
                if src_bf:
                    rhs = tsrc
                else:
                    SC.activation(stat_bf[:, e % 2, 0:width], tsrc, AF.Copy)
                    rhs = stat_bf[:, e % 2, 0:width]
                isq = SC.activation(sq_bf[:, e % 2, 0:width], tsrc, AF.Square)
                S2["sqa"].inc(isq, 1)
                S2["sqa"].wait(TE)
                if e == 0 and S2["stf"].n:
                    S2["stf"].wait(TE)
                TE.matmul(PSA[:, 0:width], ones_bf[:], rhs,
                          start=(e == 0), stop=(e == n - 1))
                i2 = TE.matmul(PSB[:, 0:width], ones_bf[:],
                               sq_bf[:, e % 2, 0:width],
                               start=(e == 0), stop=(e == n - 1))
                S2["stm"].inc(i2, 1)

        def rows_fm(S2, width):
            VE.wait_ge(S2["stm"].h, S2["stm"].n)
            VE.tensor_scalar_mul(mb_sb[:, 0:width], PSA[:, 0:width], 1.0 / D)
            VE.tensor_scalar_mul(rb_sb[:, 0:width], PSB[:, 0:width], 1.0 / D)
            VE.tensor_mul(sq_scr[:, 0, 0:width], mb_sb[:, 0:width],
                          mb_sb[:, 0:width])
            iv = VE.tensor_sub(rb_sb[:, 0:width], rb_sb[:, 0:width],
                               sq_scr[:, 0, 0:width])
            S2["row"].inc(iv, 1)
            S2["row"].wait(SC)
            isc = SC.activation(PSA[:, 0:width], rb_sb[:, 0:width],
                                AF.Sqrt, bias=eps_c[:, 0:1])
            S2["row"].inc(isc, 1)
            S2["row"].wait(VE)
            ivv = VE.reciprocal(rb_sb[:, 0:width], PSA[:, 0:width])
            S2["stf"].inc(ivv, 1)

        def attn_tile(S, x_rows, toff_spill, do_spill, carS_ap, carP_ap):
            for bs in S["mk"] + S["mu"]:
                if bs.n:
                    bs.wait(VE)
            for sub in range(NSUB1):
                if S["sq"].n:
                    SY.wait_ge(S["sq"].h, S["sq"].n)
                S["x"][0].inc(SY.dma_start(
                    out=x_tm[:, 0, :],
                    in_=x_rows[sub * 128:(sub + 1) * 128, :]), 16)
                S["x"][0].wait(SC)
                if S["tp"].n:
                    S["tp"].wait(SC)
                icast = SC.activation(xnt[:], x_tm[:, 0, :], AF.Copy)
                S["sq"].inc(icast, 1)
                S["sq"].wait(TE)
                for w in range(2):
                    if S["ev"].n:
                        S["ev"].wait(TE)
                    for j in range(8):
                        pt = w * 8 + j
                        ii = TE.transpose(PTB[:, j * 128:(j + 1) * 128],
                                          xnt[:, pt * 128:(pt + 1) * 128],
                                          idb_sb[:])
                    S["tp"].inc(ii, 1)
                    S["tp"].wait(VE)
                    iev = VE.tensor_copy(
                        xn_fm[:, w * 8:(w + 1) * 8, sub * 128:(sub + 1) * 128],
                        PTB[:, 0:1024].rearrange("p (a b) -> p a b", a=8))
                    S["ev"].inc(iev, 1)
            S["ev"].wait(SC)
            stats_fm(S, lambda e: xn_fm[:, e, :], NPT, T1, src_bf=True)
            rows_fm(S, T1)
            S["row"].wait(VE)
            for e in range(NPT):
                VE.tensor_sub(p1scr[0:128, 0:T1], xn_fm[:, e, :], mb_sb[:, 0:T1])
                i2 = VE.tensor_mul(xn_fm[:, e, :], p1scr[0:128, 0:T1],
                                   rb_sb[:, 0:T1])
                S["nrm"].inc(i2, 1)
            S["nrm"].wait(TE)
            for eg in range(8):
                gpr = eg % 2
                for el in range(2):
                    e = eg * 2 + el
                    bk = e % 2
                    pk = (PK0, PK1)[bk]
                    pu = (PU0, PU1)[bk]
                    c_dst = uni_cv[:, gpr * 2 + el, :]
                    v_dst = uni_cv[:, 4 + gpr * 2 + el, :]
                    if S["cv"].n >= 4:
                        TE.wait_ge(S["cv"].h, S["cv"].n - 3)
                    for kc in range(16):
                        ii = TE.matmul(pk[:, 0:T1],
                                       wz_sb[:, kc, e * 128:(e + 1) * 128],
                                       xn_fm[:, kc, :],
                                       start=(kc == 0), stop=(kc == 15))
                    S["mk"][bk].inc(ii, 1)
                    if S["cv"].n >= 4:
                        TE.wait_ge(S["cv"].h, S["cv"].n - 3)
                    for kc in range(16):
                        ii = TE.matmul(pu[:, 0:T1],
                                       wn_sb[:, kc, e * 128:(e + 1) * 128],
                                       xn_fm[:, kc, :],
                                       start=(kc == 0), stop=(kc == 15))
                    S["mu"][bk].inc(ii, 1)
                    S["mk"][bk].wait(SC)
                    if S["scn"].n >= 8:
                        SC.wait_ge(S["scn"].h, S["scn"].n - 4)
                    if do_spill and S["spw"].n >= 64:
                        SC.wait_ge(S["spw"].h, S["spw"].n - 32)
                    SC.activation(c_dst, pk[:, 0:T1], AF.Sigmoid,
                                  bias=nbz_sb[:, e:e + 1], scale=-1.0)
                    if S["cv"].n:
                        SC.wait_ge(S["cv"].h, S["cv"].n)
                    SC.activation(zt[0:128, 0:T1], pk[:, 0:T1], AF.Sigmoid,
                                  bias=bz_sb[:, e:e + 1])
                    S["mu"][bk].wait(SC)
                    SC.activation(rt[0:128, 0:T1], pu[:, 0:T1], AF.Relu,
                                  bias=bn_sb[:, e:e + 1])
                    SC.activation(st_[0:128, 0:T1], pu[:, 0:T1], AF.Relu,
                                  bias=nbn_sb[:, e:e + 1], scale=-1.0)
                    is2 = SC.activation(st_[0:128, 0:T1], st_[0:128, 0:T1],
                                        AF.Sigmoid, scale=-1.0)
                    S["cz"].inc(is2, 1)
                    S["cz"].wait(VE)
                    if S["scn"].n >= 8:
                        VE.wait_ge(S["scn"].h, S["scn"].n - 4)
                    if do_spill and S["spw"].n >= 64:
                        VE.wait_ge(S["spw"].h, S["spw"].n - 32)
                    VE.tensor_add(rt[0:128, 0:T1], rt[0:128, 0:T1],
                                  st_[0:128, 0:T1])
                    iv = VE.tensor_mul(v_dst, zt[0:128, 0:T1], rt[0:128, 0:T1])
                    S["cv"].inc(iv, 1)
                if do_spill:
                    S["cv"].wait(SY)
                    i1 = SY.dma_start(
                        out=c_fm[:, eg * 2:(eg + 1) * 2,
                                 toff_spill:toff_spill + T1],
                        in_=uni_cv[:, gpr * 2:gpr * 2 + 2, :])
                    S["spw"].inc(i1, 16)
                    i2 = SY.dma_start(
                        out=v_fm[:, eg * 2:(eg + 1) * 2,
                                 toff_spill:toff_spill + T1],
                        in_=uni_cv[:, 4 + gpr * 2:4 + gpr * 2 + 2, :])
                    S["spw"].inc(i2, 16)
                for el in range(2):
                    e = eg * 2 + el
                    S["cv"].wait(VE)
                    VE.tensor_tensor_scan(h_scr[0:128, 0:T1],
                                          uni_cv[:, gpr * 2 + el, :],
                                          uni_cv[:, 4 + gpr * 2 + el, :],
                                          carS_ap[:, e:e + 1], OP.mult, OP.add)
                    VE.drain()
                    VE.tensor_copy(carS_ap[:, e:e + 1],
                                   h_scr[0:128, T1 - 1:T1])
                    ic1 = VE.drain()
                    S["scn"].inc(ic1, 1)
                    VE.tensor_tensor_scan(p_scr[0:128, 0:T1],
                                          uni_cv[:, gpr * 2 + el, :],
                                          uni_cv[:, gpr * 2 + el, :],
                                          carP_ap[:, e:e + 1], OP.mult, OP.bypass)
                    VE.drain()
                    VE.tensor_copy(carP_ap[:, e:e + 1],
                                   p_scr[0:128, T1 - 1:T1])
                    isc = VE.drain()
                    S["scn"].inc(isc, 1)

        g_const.wait(TE)
        v_const.wait(SC)
        v_const.wait(VE)
        s_const.wait(SC)
        s_const.wait(TE)
        s_const.wait(SY)

        def emit_slot(t):
            for tt in range(N1):
                attn_tile(PRE, xpref[t * MAIN + tt * T1:
                                     t * MAIN + (tt + 1) * T1, :],
                          0, False, carS_pref, carP_pref)

        for t in range(NSLOT):
            passes = (("sy", SY), ("sc", SC), ("ve", VE), ("te", TE))
            snap = [s.n for s in all_sems]
            for pi, (nm, eng) in enumerate(passes):
                rnp, _ = regs[nm]
                with eng.register(f"c{t}_{nm}") as cnd:
                    eng.reg_alu(cnd, rnp, t, OP.is_gt)
                    with eng.If(cnd):
                        ctl[0] = nm
                        emit_slot(t)
                        ctl[0] = None
                nc.switch_bb(f"slot{t}_{nm}_join")
                if pi < len(passes) - 1:
                    for s, v in zip(all_sems, snap):
                        s.n = v

        for eng in (SY, SC, VE, TE, GP):
            eng.sem_inc(bar1.h, 1)
        bar1.n = 5
        for eng in (SY, SC, VE, TE, GP):
            bar1.wait(eng)

        for tt in range(N1):
            attn_tile(MN, xmain[tt * T1:(tt + 1) * T1, :], tt * T1, True,
                      carS_main, carP_main)
            if DBG and tt == 0:
                MN["ev"].wait(SY)
                dbs.inc(SY.dma_start(out=d_xn[:, :, 0:T1], in_=xn_fm[:]), 16)
                dbs.inc(SY.dma_start(out=d_col[:, 0:4], in_=colm[:, 0:4]), 16)
                dbs.inc(SY.dma_start(out=d_col[:, 8:12], in_=colr[:, 0:4]), 16)
                dbs.wait(SC)
                dbs.wait(VE)

        for eng in (SY, SC, VE, TE, GP):
            eng.sem_inc(bar2.h, 1)
        bar2.n = 5
        for eng in (SY, SC, VE, TE, GP):
            bar2.wait(eng)

        MN["spw"].wait(SY)
        MN["scn"].wait(SY)
        for eg in range(2):
            gpr = eg % 2
            i1 = SY.dma_start(
                out=uni_cv[:, gpr * 2:gpr * 2 + 2, 0:T2],
                in_=c_fm[:, eg * 2:(eg + 1) * 2, 0:T2])
            q_cv[gpr].inc(i1, 16)
            i2 = SY.dma_start(
                out=uni_cv[:, 4 + gpr * 2:4 + gpr * 2 + 2, 0:T2],
                in_=v_fm[:, eg * 2:(eg + 1) * 2, 0:T2])
            q_cv[gpr].inc(i2, 16)

        _, rpar_g = regs["gp"]
        with GP.If_eq(rpar_g, 0):
            GP.dma_start(out=sh_sum[0], in_=car[:]).then_inc(wx.h, 16)
        with GP.Else():
            GP.dma_start(out=sh_sum[1], in_=car[:]).then_inc(wx.h, 16)
        wx.n += 16
        wx.wait(GP)
        with GP.If_eq(rpar_g, 0):
            GP.dma_start(out=sh_flag[0:1, :], in_=nonce_sb[:]).then_inc(wx.h, 16)
        with GP.Else():
            GP.dma_start(out=sh_flag[1:2, :], in_=nonce_sb[:]).then_inc(wx.h, 16)
        wx.n += 16
        wx.wait(GP)
        rflg = es.enter_context(GP.register("flg"))
        rnon = es.enter_context(GP.register("non"))
        rnd = es.enter_context(GP.register("nd"))
        GP.load(rnon, nonce[0:1, 0:1])
        GP.reg_mov(rnd, 1)
        with GP.While(rnd):
            with GP.If_eq(rpar_g, 0):
                GP.load(rflg, sh_flag[1:2, 0:1])
            with GP.Else():
                GP.load(rflg, sh_flag[0:1, 0:1])
            GP.reg_alu(rnd, rflg, rnon, OP.not_equal)
        with GP.If_eq(rpar_g, 0):
            GP.dma_start(out=par[:], in_=sh_sum[1]).then_inc(wx.h, 16)
        with GP.Else():
            GP.dma_start(out=par[:], in_=sh_sum[0]).then_inc(wx.h, 16)
        wx.n += 16
        wx.wait(GP)
        GP.sem_inc(x_dn.h, 1)
        x_dn.n += 1

        if DBG:
            dbs.inc(SY.dma_start(out=d_car, in_=car[:]), 16)
            dbs.inc(SY.dma_start(out=d_c, in_=c_hbm), 16)
            dbs.inc(SY.dma_start(out=d_v, in_=v_hbm), 16)
        x_dn.wait(VE)
        _, rpar_v = regs["ve"]
        with VE.If_eq(rpar_v, 0):
            VE.tensor_mul(tmp_a, par[:, 0:16], car[:, 16:32])
            VE.drain()
            VE.tensor_add(tmp_a, tmp_a, par[:, 16:32])
            VE.drain()
            VE.tensor_mul(tmp_b, car[:, 0:16], par[:, 0:16])
            VE.drain()
            VE.tensor_scalar_mul(tmp_b, tmp_b, 0.5)
            VE.drain()
            VE.tensor_add(seed[:], tmp_a, tmp_b)
            VE.drain()
            VE.tensor_copy(car2[:], seed[:])
            VE.drain().then_inc(x_sd.h, 1)
        with VE.Else():
            VE.tensor_mul(tmp_a, car[:, 0:16], par[:, 16:32])
            VE.drain()
            VE.tensor_add(tmp_a, tmp_a, car[:, 16:32])
            VE.drain()
            VE.tensor_mul(tmp_b, par[:, 0:16], car[:, 0:16])
            VE.drain()
            VE.tensor_scalar_mul(tmp_b, tmp_b, 0.5)
            VE.drain()
            VE.tensor_add(tmp_a, tmp_a, tmp_b)
            VE.drain()
            VE.tensor_mul(tmp_a, tmp_a, par[:, 32:48])
            VE.drain()
            VE.tensor_add(seed[:], tmp_a, par[:, 48:64])
            VE.drain()
            VE.tensor_copy(car2[:], seed[:])
            VE.drain().then_inc(x_sd.h, 1)
        x_sd.n += 1
        x_sd.wait(VE)
        if DBG:
            x_sd.wait(SY)
            dbs.inc(SY.dma_start(out=d_par, in_=par[:]), 16)
            dbs.inc(SY.dma_start(out=d_seed, in_=seed[:]), 16)

        p1es.close()
        p2es = es.enter_context(contextlib.ExitStack())
        uni_h = p2es.enter_context(nc.sbuf_tensor("uni_h", [128, 16, T2], F32))
        xn3 = p2es.enter_context(nc.sbuf_tensor("xn3", [128, NPT, HALF], BF16))
        hs_sc = p2es.enter_context(nc.sbuf_tensor("hs_sc", [128, 16, HALF], BF16))
        wg_sb = p2es.enter_context(nc.sbuf_tensor("wg_sb", [128, 2, 16, 128], BF16))
        wu_sb = p2es.enter_context(nc.sbuf_tensor("wu_sb", [128, 2, 16, 128], BF16))
        wd_sb = p2es.enter_context(nc.sbuf_tensor("wd_sb", [128, 3, 16, 128], BF16))
        uni_f = p2es.enter_context(nc.sbuf_tensor("uni_f", [128, 8, T2], F32))
        mm_t = p2es.enter_context(nc.sbuf_tensor("mm_t", [128, 4, 512], F32))
        hp_t = p2es.enter_context(nc.sbuf_tensor("hp_t", [128, 8, 512], F32))
        spar = [x_tm[:, 0, 0:T2], x_tm[:, 0, T2:2 * T2]]
        xnt_f32 = xnt[:].bitcast(F32)
        silu_sl = [x_tm[:, 0, 2 * T2:3 * T2], x_tm[:, 0, 3 * T2:4 * T2],
                   xnt_f32[:, 0:T2], xnt_f32[:, T2:2 * T2]]

        def ffn_ap(ht, sub):
            idx = ht + NPT * sub
            if idx < 8:
                return uni_cv[:, idx, 0:T2]
            if idx < 24:
                return uni_h[:, idx - 8, 0:T2]
            return uni_f[:, idx - 24, 0:T2]

        out_p = [hp_t[:, 0:4, :].rearrange("p a b -> p (a b)")[:, 0:D],
                 hp_t[:, 4:8, :].rearrange("p a b -> p (a b)")[:, 0:D]]


        xn3_rdy = [0] * NS2
        for half in range(2):
            hoff = half * HALF
            b = bar3[half]
            for eng in (SY, SC, VE, TE, GP):
                eng.sem_inc(b.h, 1)
            b.n = 5
            for eng in (SY, SC, VE, TE, GP):
                b.wait(eng)

            MN["spw"].wait(SY)
            Q = dict(sqa=q_sqa, stm=q_stm, stf=q_stf, row=q_row)
            xfm2 = hs_sc[:, :, 0:T2]
            for tt in range(N2H):
                toff = hoff + tt * T2
                if q_sqa.n:
                    VE.wait_ge(q_sqa.h, q_sqa.n)
                if f_otp.n:
                    VE.wait_ge(f_otp.h, f_otp.n)
                for eg in range(8):
                    gpr = eg % 2
                    if not (half == 0 and tt == 0 and eg < 2):
                        if cv_rel[gpr] is not None:
                            SY.wait_ge(q_sc.h, cv_rel[gpr])
                        i1 = SY.dma_start(
                            out=uni_cv[:, gpr * 2:gpr * 2 + 2, 0:T2],
                            in_=c_fm[:, eg * 2:(eg + 1) * 2, toff:toff + T2])
                        q_cv[gpr].inc(i1, 16)
                        i2 = SY.dma_start(
                            out=uni_cv[:, 4 + gpr * 2:4 + gpr * 2 + 2, 0:T2],
                            in_=v_fm[:, eg * 2:(eg + 1) * 2, toff:toff + T2])
                        q_cv[gpr].inc(i2, 16)
                    q_cv[gpr].wait(VE)
                    iw = None
                    for el in range(2):
                        e = eg * 2 + el
                        VE.tensor_tensor_scan(uni_h[:, e, 0:T2],
                                              uni_cv[:, gpr * 2 + el, 0:T2],
                                              uni_cv[:, 4 + gpr * 2 + el, 0:T2],
                                              car2[:, e:e + 1], OP.mult, OP.add)
                        VE.drain()
                        VE.tensor_copy(car2[:, e:e + 1],
                                       uni_h[:, e, T2 - 1:T2])
                        iw = VE.drain()
                        q_sc.inc(iw, 1)
                    cv_rel[gpr] = q_sc.n
                for sub in range(T2 // 128):
                    if q_cst.n:
                        SY.wait_ge(q_cst.h, q_cst.n)
                    ix = SY.dma_start(
                        out=x_tm[:, 0, :],
                        in_=xmain[toff + sub * 128:toff + (sub + 1) * 128, :])
                    q_x[0].inc(ix, 16)
                    q_x[0].wait(SC)
                    if q_tp.n:
                        q_tp.wait(SC)
                    if q_mid.n and sub == 0:
                        SC.wait_ge(q_mid.h, q_mid.n)
                    icast = SC.activation(xnt[:], x_tm[:, 0, :], AF.Copy)
                    q_cst.inc(icast, 1)
                    q_cst.wait(TE)
                    for w in range(2):
                        if q_ev.n:
                            q_ev.wait(TE)
                        for j in range(8):
                            pt = w * 8 + j
                            ii = TE.transpose(PTB[:, j * 128:(j + 1) * 128],
                                              xnt[:, pt * 128:(pt + 1) * 128],
                                              idb_sb[:])
                        q_tp.inc(ii, 1)
                        q_tp.wait(VE)
                        iev = VE.tensor_copy(
                            xfm2[:, w * 8:(w + 1) * 8,
                                 sub * 128:(sub + 1) * 128],
                            PTB[:, 0:1024].rearrange("p (a b) -> p a b", a=8))
                        q_ev.inc(iev, 1)
                q_sc.wait(SC)
                stats_fm(Q, lambda e: uni_h[:, e, 0:T2], NPT, T2, src_bf=False)
                rows_fm(Q, T2)
                q_row.wait(VE)
                q_ev.wait(VE)
                imid = None
                for e in range(NPT):
                    VE.tensor_sub(uni_h[:, e, 0:T2], uni_h[:, e, 0:T2],
                                  mb_sb[:, 0:T2])
                    VE.tensor_mul(uni_h[:, e, 0:T2], uni_h[:, e, 0:T2],
                                  rb_sb[:, 0:T2])
                    imid = VE.tensor_add(uni_h[:, e, 0:T2], uni_h[:, e, 0:T2],
                                         xfm2[:, e, 0:T2])
                    q_mid.inc(imid, 1)
                q_mid.wait(SY)
                ixm = SY.dma_start(out=xm_fm[:, :, toff:toff + T2],
                                   in_=uni_h[:, :, 0:T2])
                q_xm.inc(ixm, 16)
                q_mid.wait(SC)
                stats_fm(Q, lambda e: uni_h[:, e, 0:T2], NPT, T2, src_bf=False)
                rows_fm(Q, T2)
                q_row.wait(VE)
                i3 = None
                for e in range(NPT):
                    VE.tensor_sub(mm_t[:, 0, 0:T2], uni_h[:, e, 0:T2],
                                  mb_sb[:, 0:T2])
                    i3 = VE.tensor_mul(xn3[:, e, tt * T2:(tt + 1) * T2],
                                       mm_t[:, 0, 0:T2], rb_sb[:, 0:T2])
                    q_ln3.inc(i3, 1)
                xn3_rdy[tt] = q_ln3.n

            if f_ost.n:
                VE.wait_ge(f_ost.h, f_ost.n)
            bank_free["PSA"] = (q_stf, q_stf.n)
            bank_free["PSB"] = (q_stf, q_stf.n)
            GB = ((PK0, "PK0"), (PK1, "PK1"), (PSA, "PSA"))
            UB = ((PU0, "PU0"), (PU1, "PU1"), (PSB, "PSB"))
            DB = ((PK0, "PK0"), (PK1, "PK1"), (PU0, "PU0"))
            for sc in range(4):
                hs_base = f_hs.n
                for pre in range(3):
                    if wd_rel[pre] is not None:
                        GP.wait_ge(wd_rel[pre][0].h, wd_rel[pre][1])
                    idw = GP.dma_start(out=wd_sb[:, pre], in_=wdt[sc, pre])
                    f_wd[pre % 2].inc(idw, 16)
                    wd_ld[pre] = (f_wd[pre % 2], f_wd[pre % 2].n)
                for qb in range(4):
                    for r in range(4):
                        gi = (sc * 4 + qb) * 4 + r
                        ci = gi % 2
                        pg, gnm = GB[gi % 3]
                        pu, unm = UB[gi % 3]
                        if wg_rel[ci] is not None:
                            GP.wait_ge(wg_rel[ci][0].h, wg_rel[ci][1])
                        ig = GP.dma_start(
                            out=wg_sb[:, ci],
                            in_=wgt[sc, qb, r])
                        f_wg[ci].inc(ig, 16)
                        if wu_rel[ci] is not None:
                            GP.wait_ge(wu_rel[ci][0].h, wu_rel[ci][1])
                        iu = GP.dma_start(
                            out=wu_sb[:, ci],
                            in_=wut[sc, qb, r])
                        f_wu[ci].inc(iu, 16)
                        for sub in range(NS2):
                            slt = (gi % 2) * NS2 + sub
                            TE.wait_ge(q_ln3.h, xn3_rdy[sub])
                            f_wg[ci].wait(TE)
                            bw = bank_free.get(gnm)
                            if bw is not None:
                                TE.wait_ge(bw[0].h, bw[1])
                            for kc in range(16):
                                ii = TE.matmul(
                                    pg[:, 0:T2], wg_sb[:, ci, kc, :],
                                    xn3[:, kc, sub * T2:(sub + 1) * T2],
                                    start=(kc == 0), stop=(kc == 15))
                            f_mgA.inc(ii, 1)
                            mg_rec = f_mgA.n
                            if sub == NS2 - 1:
                                wg_rel[ci] = (f_mgA, f_mgA.n)
                            f_wu[ci].wait(TE)
                            bw = bank_free.get(unm)
                            if bw is not None:
                                TE.wait_ge(bw[0].h, bw[1])
                            for kc in range(16):
                                ii = TE.matmul(
                                    pu[:, 0:T2], wu_sb[:, ci, kc, :],
                                    xn3[:, kc, sub * T2:(sub + 1) * T2],
                                    start=(kc == 0), stop=(kc == 15))
                            f_mupA.inc(ii, 1)
                            mu_rec = f_mupA.n
                            if sub == NS2 - 1:
                                wu_rel[ci] = (f_mupA, f_mupA.n)
                            SC.wait_ge(f_mgA.h, mg_rec)
                            sw = silu_rel[slt]
                            if sw is not None:
                                SC.wait_ge(sw[0].h, sw[1])
                            isl = SC.activation(silu_sl[slt],
                                                pg[:, 0:T2], AF.Silu)
                            f_sil.inc(isl, 1)
                            bank_free[gnm] = (f_sil, f_sil.n)
                            f_sil.wait(VE)
                            VE.wait_ge(f_mupA.h, mu_rec)
                            ihp = VE.tensor_mul(
                                hp_t[:, r * NS2 + sub, 0:T2],
                                pu[:, 0:T2], silu_sl[slt])
                            f_hp.inc(ihp, 1)
                            silu_rel[slt] = (f_hp, f_hp.n)
                            bank_free[unm] = (f_hp, f_hp.n)
                    for sub in range(NS2):
                        f_hp.wait(VE)
                        if f_hs.n:
                            VE.wait_ge(f_hs.h, f_hs.n)
                        if q_stm.n:
                            VE.wait_ge(q_stm.h, q_stm.n)
                        for r in range(4):
                            VE.tensor_mul(mm_t[:, r, 0:T2],
                                          hp_t[:, r * NS2 + sub, 0:T2],
                                          hp_t[:, r * NS2 + sub, 0:T2])
                        VE.tensor_max(spar[0], mm_t[:, 0, 0:T2],
                                      mm_t[:, 1, 0:T2])
                        VE.tensor_tensor(spar[1], mm_t[:, 0, 0:T2],
                                         mm_t[:, 1, 0:T2], OP.min)
                        VE.tensor_max(sq_scr[:, 0, 0:T2], mm_t[:, 2, 0:T2],
                                      mm_t[:, 3, 0:T2])
                        VE.tensor_tensor(sq_scr[:, 1, 0:T2], mm_t[:, 2, 0:T2],
                                         mm_t[:, 3, 0:T2], OP.min)
                        VE.tensor_tensor(spar[0], spar[0],
                                         sq_scr[:, 0, 0:T2], OP.min)
                        VE.tensor_max(spar[1], spar[1],
                                      sq_scr[:, 1, 0:T2])
                        VE.tensor_max(spar[0], spar[0],
                                      spar[1])
                        ihs = None
                        for r in range(4):
                            VE.tensor_tensor(sq_scr[:, 1, 0:T2],
                                             mm_t[:, r, 0:T2],
                                             spar[0], OP.is_ge)
                            ihs = VE.tensor_mul(
                                hs_sc[:, qb * 4 + r, sub * T2:(sub + 1) * T2],
                                hp_t[:, r * NS2 + sub, 0:T2],
                                sq_scr[:, 1, 0:T2])
                        f_hs.inc(ihs, 1)
                for ht in range(16):
                    s4 = ht % 3
                    TE.wait_ge(wd_ld[s4][0].h, wd_ld[s4][1])
                    for sub in range(NS2):
                        di = (sc * 16 + ht) * NS2 + sub
                        pd, dnm = DB[di % 3]
                        bw = bank_free.get(dnm)
                        if bw is not None:
                            TE.wait_ge(bw[0].h, bw[1])
                        if ht == 0 and sub == 0:
                            TE.wait_ge(f_hs.h, hs_base + 2 * NS2)
                            for s in range(8):
                                TE.matmul(
                                    pd[:, 0:T2], wd_sb[:, s4, s, :],
                                    hs_sc[:, s, sub * T2:(sub + 1) * T2],
                                    start=(s == 0), stop=False)
                            TE.wait_ge(f_hs.h, hs_base + 4 * NS2)
                            for s in range(8, 16):
                                ii = TE.matmul(
                                    pd[:, 0:T2], wd_sb[:, s4, s, :],
                                    hs_sc[:, s, sub * T2:(sub + 1) * T2],
                                    start=False, stop=(s == 15))
                        else:
                            for s in range(16):
                                ii = TE.matmul(
                                    pd[:, 0:T2], wd_sb[:, s4, s, :],
                                    hs_sc[:, s, sub * T2:(sub + 1) * T2],
                                    start=(s == 0), stop=(s == 15))
                        f_mdA.inc(ii, 1)
                        md_rec = f_mdA.n
                        VE.wait_ge(f_mdA.h, md_rec)
                        if sc == 0:
                            iac = VE.tensor_copy(ffn_ap(ht, sub),
                                                 pd[:, 0:T2])
                        else:
                            iac = VE.tensor_add(ffn_ap(ht, sub),
                                                pd[:, 0:T2], ffn_ap(ht, sub))
                        f_fdA.inc(iac, 1)
                        bank_free[dnm] = (f_fdA, f_fdA.n)
                    wd_rel[s4] = (f_mdA, f_mdA.n)
                    if ht + 3 < 16:
                        GP.wait_ge(wd_rel[s4][0].h, wd_rel[s4][1])
                        idw = GP.dma_start(out=wd_sb[:, s4],
                                           in_=wdt[sc, ht + 3])
                        f_wd[s4 % 2].inc(idw, 16)
                        wd_ld[s4] = (f_wd[s4 % 2], f_wd[s4 % 2].n)

            q_xm.wait(GP)
            for nm in ("PSA", "PSB"):
                bw = bank_free.get(nm)
                if bw is not None:
                    TE.wait_ge(bw[0].h, bw[1])
            for sub in range(NS2):
                f_fdA.wait(SC)
                stats_fm(Q, lambda e: ffn_ap(e, sub), NPT, T2, src_bf=False)
                rows_fm(Q, T2)
                q_row.wait(VE)
                i2 = None
                for e in range(NPT):
                    VE.tensor_sub(ffn_ap(e, sub), ffn_ap(e, sub),
                                  mb_sb[:, 0:T2])
                    i2 = VE.tensor_mul(ffn_ap(e, sub), ffn_ap(e, sub),
                                       rb_sb[:, 0:T2])
                    f_nrm4.inc(i2, 1)
                f_nrm4.wait(GP)
                idx0 = NPT * sub
                tcol0 = hoff + sub * T2
                if idx0 < 8:
                    ia = GP.dma_start(out=uni_cv[:, 0:8, 0:T2],
                                      in_=xm_fm[:, 0:8, tcol0:tcol0 + T2],
                                      accum_op=OP.add)
                    f_xr.inc(ia, 16)
                    ib = GP.dma_start(out=uni_h[:, 0:8, 0:T2],
                                      in_=xm_fm[:, 8:16, tcol0:tcol0 + T2],
                                      accum_op=OP.add)
                    f_xr.inc(ib, 16)
                else:
                    ia = GP.dma_start(out=uni_h[:, 8:16, 0:T2],
                                      in_=xm_fm[:, 0:8, tcol0:tcol0 + T2],
                                      accum_op=OP.add)
                    f_xr.inc(ia, 16)
                    ib = GP.dma_start(out=uni_f[:, 0:8, 0:T2],
                                      in_=xm_fm[:, 8:16, tcol0:tcol0 + T2],
                                      accum_op=OP.add)
                    f_xr.inc(ib, 16)
                for s2 in range(T2 // 128):
                    gidx = (half * NS2 + sub) * (T2 // 128) + s2
                    tok0 = hoff + sub * T2 + s2 * 128
                    pr = gidx % 2
                    f_xr.wait(TE)
                    q_ev.wait(TE)
                    for w in range(4):
                        if f_oev.n:
                            f_oev.wait(TE)
                        for j in range(4):
                            pt = w * 4 + j
                            ii = TE.transpose(
                                PTF[:, j * 128:(j + 1) * 128],
                                ffn_ap(pt, sub)[:, s2 * 128:(s2 + 1) * 128],
                                idf_sb[:])
                        f_otp.inc(ii, 1)
                        f_otp.wait(VE)
                        if w == 0 and f_ost.n >= 32:
                            VE.wait_ge(f_ost.h, f_ost.n - 16)
                        iev = VE.tensor_copy(
                            out_p[pr].rearrange("p (a b) -> p a b", a=16)
                            [:, w * 4:(w + 1) * 4, :],
                            PTF[:, 0:512].rearrange("p (a b) -> p a b", a=4))
                        f_oev.inc(iev, 1)
                    f_oev.wait(SC)
                    ist = SC.dma_start(out=y[tok0:tok0 + 128, :], in_=out_p[pr])
                    f_ost.inc(ist, 16)

        f_ost.wait(SC)
        for eng in (SY, VE, TE, GP):
            f_ost.wait(eng)

    return nc


def _plane_perm():
    q = np.arange(2048)
    perm = np.empty(8192, np.int64)
    for r in range(4):
        perm[r * 2048:(r + 1) * 2048] = q * 4 + r
    return perm


def _prep_weights(Wz, Wn, wg, wu, wd):
    wzT = np.ascontiguousarray(Wz.T)
    wnT = np.ascontiguousarray(Wn.T)
    perm = _plane_perm()
    wgp = wg[:, perm]
    wup = wu[:, perm]
    wdp = wd[perm, :]

    def strips_gate(w):
        w4 = w.reshape(16, 128, 4, 16, 128)
        w4 = w4.transpose(2, 3, 1, 0, 4)
        w4 = w4.reshape(4, 4, 4, 128, 16, 128)
        w4 = w4.transpose(1, 2, 0, 3, 4, 5)
        return np.ascontiguousarray(w4)

    wgt = strips_gate(wgp)
    wut = strips_gate(wup)
    wdr = wdp.reshape(4, 16, 128, 16, 128)
    wdr = wdr.transpose(1, 0, 2, 3, 4)
    wdr = wdr.reshape(4, 4, 4, 128, 16, 128)
    wdr = wdr.transpose(0, 4, 3, 1, 2, 5)
    wdt = np.ascontiguousarray(wdr.reshape(4, 16, 128, 16, 128))
    return wzT, wnT, wgt, wut, wdt


def _build_and_maps(x2d, Wz, bz, Wn, bn, wg, wu, wd, MAIN):
    L = x2d.shape[0]
    assert L == NCORES * MAIN
    nc = build_nc(MAIN)
    wzT, wnT, wgt, wut, wdt = _prep_weights(Wz, Wn, wg, wu, wd)
    bzt = np.ascontiguousarray(bz.reshape(NPT, 128).T)
    bnt = np.ascontiguousarray(bn.reshape(NPT, 128).T)
    ones = np.ones((128, 128), np.float32)
    idf = np.eye(128, dtype=np.float32)
    idb = np.eye(128, dtype=ml_dtypes.bfloat16)
    nonce_v = np.random.randint(1, 2**31 - 1, dtype=np.int32)
    nonce_arr = np.full((1, 16), nonce_v, np.int32)

    in_maps = []
    for k in range(NCORES):
        j, b = k // 2, k % 2
        xmain = np.ascontiguousarray(x2d[k * MAIN:(k + 1) * MAIN])
        xpref = np.zeros((NSLOT * MAIN, D), np.float32)
        start = b * j * MAIN
        if j > 0:
            xpref[0:j * MAIN] = x2d[start:start + j * MAIN]
        meta_arr = np.array([[k, j, b, 0]], np.int32)
        in_maps.append({
            "xmain": xmain, "xpref": xpref,
            "wzT": wzT, "wnT": wnT, "wgt": wgt, "wut": wut, "wdt": wdt,
            "bz_in": bzt, "bn_in": bnt, "ones_in": ones,
            "idf_in": idf, "idb_in": idb, "meta": meta_arr,
            "nonce": nonce_arr,
        })
    return nc, in_maps


def _run(x2d, Wz, bz, Wn, bn, wg, wu, wd, MAIN):
    L = x2d.shape[0]
    nc, in_maps = _build_and_maps(x2d, Wz, bz, Wn, bn, wg, wu, wd, MAIN)
    res = run_bass_kernel_spmd(nc, in_maps, core_ids=list(range(NCORES)))
    out = np.empty((L, D), np.float32)
    for k in range(NCORES):
        out[k * MAIN:(k + 1) * MAIN] = res.results[k]["y"]
    return out


def kernel(L0_attn_x, L0_attn_Wz, L0_attn_bz, L0_attn_Wn, L0_attn_bn,
           L0_ffn_w_gate, L0_ffn_w_up, L0_ffn_w_down):
    x = np.asarray(L0_attn_x, np.float32)
    B, L, _ = x.shape
    x2d = np.ascontiguousarray(x.reshape(B * L, D))
    out = _run(x2d,
               np.asarray(L0_attn_Wz, np.float32),
               np.asarray(L0_attn_bz, np.float32),
               np.asarray(L0_attn_Wn, np.float32),
               np.asarray(L0_attn_bn, np.float32),
               np.asarray(L0_ffn_w_gate, np.float32),
               np.asarray(L0_ffn_w_up, np.float32),
               np.asarray(L0_ffn_w_down, np.float32),
               MAIN=(B * L) // NCORES)
    return out.reshape(B, L, D)
